# version 43
# baseline (speedup 1.0000x reference)
import contextlib
import numpy as np
import concourse.bass as bass
import concourse.mybir as mybir
from concourse.bass_utils import run_bass_kernel_spmd

F32 = mybir.dt.float32
BF16 = mybir.dt.bfloat16
AF = mybir.ActivationFunctionType
ALU = mybir.AluOpType
AXX = mybir.AxisListType.X

P = 128
D = 1024
NH = D // P
NBD = D // 512
EPS = 1e-5
MAXD = 10
BSCALE = 0.01
BW = P + 2 * MAXD - 2
KiB = 1024


class SemCtr:
    def __init__(self, h, name):
        self.h = h
        self.name = name
        self.v = 0

    def bump(self, amt=1):
        self.v += amt
        return self.v


class Prog:
    ENGINES = ("tensor", "vector", "scalar", "sync", "gpsimd")

    def __init__(self):
        self.streams = {e: [] for e in self.ENGINES}
        self.all_waits = []

    def add(self, eng, fn, waits=(), inc=None, inc_amt=1):
        waits = [(s, v) for (s, v) in waits if v > 0]
        self.all_waits.extend(waits)
        post = inc.bump(inc_amt) if inc is not None else None
        self.streams[eng].append((fn, waits, inc, inc_amt, post))
        return post

    def raw_wait(self, eng, sem, val):
        self.streams[eng].append((lambda e: None, [(sem, val)], None, 1, None))

    def emit(self, eng_obj, name, guard_sem=None):
        guard = None
        if guard_sem is not None:
            guard = _ChainGuard(eng_obj, guard_sem)
            eng_obj = guard
        waited = {}
        for fn, waits, inc, inc_amt, post in self.streams[name]:
            for (s, v) in waits:
                if waited.get(s.name, -1) >= v:
                    continue
                waited[s.name] = v
                eng_obj.wait_ge(s.h, v)
            inst = fn(eng_obj)
            if inc is not None:
                assert inst is not None
                if guard is not None and inc_amt != 16:
                    guard.commit_group(inst, inc, post)
                else:
                    inst.then_inc(inc.h, inc_amt)

    def check(self):
        for (s, v) in self.all_waits:
            assert v <= s.v, f"wait {s.name}>={v} exceeds final {s.v}"


class _ChainGuard:

    def __init__(self, eng, sem):
        self._eng = eng
        self._sem = sem
        self._n = 0
        self._prev = None
        self._marker = None

    def _flush_prev(self):
        if self._prev is not None:
            self._prev.then_inc(self._sem.h, 1)
            self._n += 1
            self._marker = (self._sem.h, self._n)
            self._prev = None

    def commit_group(self, inst, inc, post_val):
        assert inst is self._prev or self._prev is None
        self._prev = None
        inst.then_inc(inc.h, 1)
        self._marker = (inc.h, post_val)

    def wait_ge(self, *a, **k):
        return self._eng.wait_ge(*a, **k)

    def dma_start(self, *a, **k):
        return self._eng.dma_start(*a, **k)

    def __getattr__(self, name):
        real = getattr(self._eng, name)
        if not callable(real):
            return real

        def wrap(*a, **k):
            self._flush_prev()
            if self._marker is not None:
                self._eng.wait_ge(self._marker[0], self._marker[1])
            inst = real(*a, **k)
            self._prev = inst
            return inst
        return wrap


def build_nc(S, sim_mode=False):
    NST = S // P
    HALF = S // 2
    NSH = NST // 2
    KW = min(512, HALF)
    NKB = HALF // KW
    assert NKB <= 4
    CHW = min(512, HALF)
    NHH = HALF // CHW
    SPW = NSH // NHH

    nc = bass.Bass("TRN2", target_bir_lowering=False,
                   detect_race_conditions=sim_mode)

    x_d = nc.dram_tensor("x", [S, D], F32, kind="ExternalInput").ap()
    wt_d = nc.dram_tensor("wt", [D, D], F32, kind="ExternalInput").ap()
    b_d = nc.dram_tensor("b", [D], F32, kind="ExternalInput").ap()
    ft_d = nc.dram_tensor("ft", [D, D], F32, kind="ExternalInput").ap()
    out_d = nc.dram_tensor("out", [S, D], F32, kind="ExternalOutput").ap()

    off = {}
    cur = 0

    def place(name, nbytes, align=512):
        nonlocal cur
        cur = (cur + align - 1) // align * align
        off[name] = cur
        cur += nbytes

    place("fT", NH * S * 2)
    f_base = cur
    place("f", NST * D * 2)
    e_base = cur
    place("e", S * 4)
    pT_base = cur
    place("pT", S * 2)
    place("p", S * 2)
    if S >= 4096:
        off["wstage"] = e_base
        off["WT"] = f_base + 32 * KiB
        off["ln1T0"] = f_base
        off["wtstage"] = f_base + 48 * KiB
        off["ln1Tr"] = f_base + 48 * KiB
        assert NH * HALF * 2 <= 32 * KiB and 64 * KiB <= NST * D * 2
        assert 2 * NH * CHW * 2 <= 16 * KiB
    else:
        place("wstage", NH * D * 4)
        place("WT", NH * D * 2)
        place("wtstage", 4 * D * 4)
        place("ln1T0", NH * HALF * 2)
        place("ln1Tr", 2 * NH * CHW * 2)
    place("xbuf0", D * 4)
    place("xbuf1", D * 4)
    lnbuf_base = cur
    place("lnbuf0", D * 2)
    place("lnbuf1", D * 2)
    place("out1a", D * 4)
    place("outbuf", D * 4)
    place("ln2", D * 2)
    place("ln2T", D * 2)
    place("wqT", NH * D * 2)
    off["wtstage2"] = off["wqT"]
    assert 4 * D * 4 <= NH * D * 2
    place("b_sb", NH * 4)
    place("consts", 8 * 4)
    place("stats", 48 * 4)
    place("id", P * 2)
    place("band", BW * 4)
    place("iotascratch", BW * 4)
    place("ones", P * 4)
    place("sqscr", D * 2)
    off["out1b"] = lnbuf_base
    ARENA_BASE = 16512
    assert ARENA_BASE + cur <= 229344, f"arena overflow: {cur}"

    with contextlib.ExitStack() as ctx:
        pb = [ctx.enter_context(nc.psum_tensor(f"pb{i}", [P, 512], F32))
              for i in range(6)]
        pbt = [ctx.enter_context(nc.psum_tensor(f"pbt{i}", [P, 1024], BF16))
               for i in range(2)]

        def view(name, cols, dt):
            h = nc.alloc_sbuf_tensor_at(name, [P, cols], dt,
                                        offset=ARENA_BASE + off[name])
            return h[:, :]

        fT = view("fT", NH * S, BF16)
        f_sb = view("f", NST * D, BF16)
        wstage = view("wstage", NH * D, F32)
        wtstage = view("wtstage", 4 * D, F32)
        wtstage2 = view("wtstage2", 4 * D, F32)
        WT = view("WT", NH * D, BF16)
        ln1T0 = view("ln1T0", NH * HALF, BF16)
        ln1Tr = view("ln1Tr", 2 * NH * CHW, BF16)
        e_sb = view("e", S, F32)
        pT_sb = view("pT", S, BF16)
        p_sb = view("p", S, BF16)
        xb = [view("xbuf0", D, F32), view("xbuf1", D, F32)]
        lnb = [view("lnbuf0", D, BF16), view("lnbuf1", D, BF16)]
        out1 = [view("out1a", D, F32), view("out1b", D, F32)]
        outb = view("outbuf", D, F32)
        xb4 = xb + [out1[0], outb]
        sqscr = view("sqscr", D, BF16)
        ln2 = view("ln2", D, BF16)
        ln2T = view("ln2T", D, BF16)
        wqT = view("wqT", NH * D, BF16)
        b_sb = view("b_sb", NH, F32)
        consts = view("consts", 8, F32)
        stats = view("stats", 48, F32)
        id_sb = view("id", P, BF16)
        band = view("band", BW, F32)
        iosc = view("iotascratch", BW, F32)
        ones_sb = view("ones", P, F32)

        SSUM, NMEAN, VARSUM, STD, VP, R0, T1, RSTD = range(8)
        LSUM0, LSUM1, RL, SCALE, HTH, NHTH, ABSP, TOT = range(16, 24)
        PM0 = 24
        NMAX0, NMAX1, ROWI, ROWB = 32, 33, 34, 35
        ABS8 = 40

        def st_(c):
            return stats[:, c:c + 1]

        twb = [pbt[0][:, 0:1024], pbt[1][:, 0:1024]]
        fw_ps = [pb[4][:, 0:512], pb[5][:, 0:512]]

        def mksem(name):
            return SemCtr(ctx.enter_context(nc.semaphore(name)), name)

        dsem_x = [mksem("dx0"), mksem("dx1"), mksem("dx2"), mksem("dx3")]
        dsem_xg = [mksem("dxg1"), mksem("dxg3")]
        dsem_ft = [mksem(f"dft{i}") for i in range(4)]
        dsem_b = mksem("dbb")
        dsem_w = [mksem(f"dw{i}") for i in range(8)]
        dsem_o = mksem("do")
        s_t = mksem("st")
        s_vp = mksem("svp")
        s_ve = mksem("sve")
        s_vo = mksem("svo")
        s_vm = mksem("svm")
        s_vln = mksem("svln")
        s_vmax = mksem("svmax")
        s_vf = mksem("svf")
        s_ss = mksem("sss")
        s_sx = mksem("ssx")
        s_gp = mksem("sgp")
        s_vq = mksem("svq")
        s_sp2 = mksem("ssp2")

        gsems = {}
        for e in ("vector", "scalar", "gpsimd", "sync"):
            gsems[e] = mksem("guard_" + e)
        if sim_mode:
            gsems["tensor"] = mksem("guard_tensor")

        PR = Prog()

        CONST_VALS = [0.0, 1.0, EPS, 1.0 / D, -0.5]
        for ci, cv in enumerate(CONST_VALS):
            nc.const_aps.register(F32, cv, consts[:, ci:ci + 1])

        def v_consts(eng):
            for ci, cv in enumerate(CONST_VALS):
                eng.memset(consts[:, ci:ci + 1], cv)
            return eng.memset(ones_sb[:, 0:P], 1.0)
        PR.add("vector", v_consts, inc=s_vq)

        def g_iota(eng):
            eng.iota(iosc[:, 0:P], [[1, P]], channel_multiplier=0,
                     allow_small_or_imprecise_dtypes=True)
            eng.iota(st_(ROWI), [[1, 1]], channel_multiplier=1,
                     allow_small_or_imprecise_dtypes=True)
            eng.iota(band[:, 0:BW], [[1, BW]], channel_multiplier=0,
                     allow_small_or_imprecise_dtypes=True)
            return eng.iota(st_(ROWB), [[1, 1]], base=MAXD - 1,
                            channel_multiplier=1,
                            allow_small_or_imprecise_dtypes=True)
        g_iota_done = PR.add("gpsimd", g_iota, inc=s_gp)

        def g_b(eng):
            return eng.dma_start(b_sb[:, 0:NH],
                                 b_d.rearrange("(c p) -> p c", p=P),
                                 allow_slow_non_contiguous=True)
        b_loaded = PR.add("gpsimd", g_b, inc=dsem_b, inc_amt=16)

        ftg_waits = {}

        def emit_ft_issue(c):
            def g_ft(eng, c=c):
                return eng.dma_start(wstage[:, c * D:(c + 1) * D],
                                     ft_d[c * P:(c + 1) * P, :])
            sem = dsem_ft[c // 2]
            PR.add("gpsimd" if c < 4 else "scalar", g_ft,
                   inc=sem, inc_amt=16)
            if c % 2 == 1:
                ftg_waits[c // 2] = (sem, sem.v)

        def v_id(eng):
            eng.tensor_scalar(id_sb[:, 0:P], iosc[:, 0:P], st_(ROWI), None,
                              ALU.is_equal)
            eng.tensor_scalar(band[:, 0:BW], band[:, 0:BW], st_(ROWB), None,
                              ALU.subtract)
            eng.tensor_scalar_add(iosc[:, 0:BW], band[:, 0:BW], float(MAXD))
            eng.tensor_scalar(band[:, 0:BW], band[:, 0:BW], -1.0,
                              float(MAXD), ALU.mult, ALU.add)
            eng.tensor_tensor(band[:, 0:BW], band[:, 0:BW], iosc[:, 0:BW],
                              ALU.min)
            eng.tensor_scalar_max(band[:, 0:BW], band[:, 0:BW], 0.0)
            return eng.tensor_scalar_mul(band[:, 0:BW], band[:, 0:BW],
                                         -BSCALE)
        id_done = PR.add("vector", v_id, waits=[(s_gp, g_iota_done)], inc=s_vq)

        def emit_warm(n, bank=5, waits=()):
            def t_warm(eng, n=n, bank=bank):
                inst = None
                for _ in range(n):
                    inst = eng.matmul(pb[bank][:, 0:P], ones_sb[:, 0:P],
                                      ones_sb[:, 0:P], start=True, stop=True)
                return inst
            PR.add("tensor", t_warm, waits=list(waits), inc=s_t)

        wprep = {}
        cast_ms = {}
        wt_lands = {}

        def wt_slot_ap(c):
            stg = wtstage if c < 4 else wtstage2
            sl = c % 4
            return stg[:, sl * D:(sl + 1) * D]

        def emit_wt_issue(c):
            sem = dsem_w[c]

            def fn(eng, c=c):
                return eng.dma_start(wt_slot_ap(c),
                                     wt_d[c * P:(c + 1) * P, :])
            PR.add("scalar" if c % 2 == 0 else "gpsimd", fn,
                   inc=sem, inc_amt=16)
            wt_lands[c] = (sem, sem.v)

        def emit_cast_c(c):
            def fn(eng, c=c):
                return eng.activation(WT[:, c * D:(c + 1) * D],
                                      wt_slot_ap(c),
                                      AF.Identity, bias=0.0, scale=1.0)
            cast_ms[c] = PR.add("scalar", fn, waits=[wt_lands[c]], inc=s_ss)
            if c == NH - 1:
                wprep["cast_last"] = cast_ms[c]

        absum_ms = {}

        def emit_absum_g(g):
            def v_absum(eng, g=g):
                inst = None
                for c in (2 * g, 2 * g + 1):
                    inst = eng.reduce_sum(st_(ABS8 + c),
                                          wstage[:, c * D:(c + 1) * D], AXX,
                                          apply_absolute_value=True)
                return inst
            absum_ms[g] = PR.add("vector", v_absum, waits=[ftg_waits[g]],
                                 inc=s_vq)

        def emit_allred_g(g, first=False, last=False):
            def t_allred(eng, g=g):
                return eng.matmul(pb[4][:, 0:2], ones_sb[:, 0:P],
                                  stats[:, ABS8 + 2 * g:ABS8 + 2 * g + 2],
                                  start=first, stop=last)
            done = PR.add("tensor", t_allred,
                          waits=[(s_vq, absum_ms[g])], inc=s_t)
            if last:
                wprep["allred"] = done

        def emit_qpre():
            def v_qpre(eng):
                eng.tensor_copy(stats[:, ABS8:ABS8 + 2], pb[4][:, 0:2])
                eng.reduce_sum(st_(TOT), stats[:, ABS8:ABS8 + 2], AXX)
                eng.tensor_scalar(st_(SCALE), st_(TOT), 1.0 / (D * D), EPS,
                                  ALU.mult, ALU.max)
                eng.tensor_scalar_mul(st_(HTH), st_(SCALE), 0.5)
                return eng.tensor_scalar_mul(st_(NHTH), st_(SCALE), -0.5)
            wprep["quant"] = PR.add("vector", v_qpre,
                                    waits=[(s_t, wprep["allred"])], inc=s_vq)

        def emit_quant_c(c):
            def v_q(eng, c=c):
                eng.tensor_single_scalar(wqT[:, c * D:(c + 1) * D],
                                         wstage[:, c * D:(c + 1) * D],
                                         st_(NHTH), ALU.is_lt)
                return eng.scalar_tensor_tensor(
                    wqT[:, c * D:(c + 1) * D],
                    wstage[:, c * D:(c + 1) * D], st_(HTH),
                    wqT[:, c * D:(c + 1) * D], ALU.is_gt, ALU.subtract)
            waits = [(s_ss, wprep["cast_last"])] \
                if not wprep.get("quant_first") else []
            done = PR.add("vector", v_q, waits=waits, inc=s_vq)
            wprep["quant_first"] = True
            if c == 3:
                wprep["quant_e"] = done
            if c == 7:
                wprep["quant_p"] = done
            if c == 5:
                wprep["quant_last"] = done

        def x_load(k, waits, nbuf=2):
            def fn(eng, k=k, nbuf=nbuf):
                st = k % NST
                buf = xb4[k % nbuf] if nbuf == 4 else xb[k % 2]
                return eng.dma_start(buf[:, :], x_d[st * P:(st + 1) * P, :])
            if nbuf == 4 and k % 2 == 1 and k >= 9:
                eng_name = "gpsimd"
                sem = dsem_xg[(k % 4) // 2]
            else:
                eng_name = "sync"
                sem = dsem_x[k % nbuf]
            PR.add(eng_name, fn, waits=waits, inc=sem, inc_amt=16)
            return (sem, sem.v)

        def ln_stat(par, xsrc, sq_scr, stat_waits):
            c0 = 8 * par

            def v_stat(eng):
                eng.reduce_sum(st_(c0 + SSUM), xsrc, AXX)
                return eng.tensor_scalar_mul(st_(c0 + NMEAN), st_(c0 + SSUM),
                                             -1.0 / D)
            vm = PR.add("vector", v_stat, waits=stat_waits, inc=s_vm)

            def s_var(eng):
                eng.activation(sq_scr, xsrc, AF.Square,
                               bias=st_(c0 + NMEAN), scale=1.0,
                               accum_out=st_(c0 + VARSUM))
                eng.activation(st_(c0 + STD), st_(c0 + VARSUM),
                               AF.Ln, bias=EPS, scale=1.0 / D)
                return eng.activation(st_(c0 + R0), st_(c0 + STD),
                                      AF.Exp, bias=0.0, scale=-0.5)
            return PR.add("scalar", s_var, waits=[(s_vm, vm)], inc=s_ss)

        def ln_apply(par, xsrc, dst_bf16, ss, apply_waits):
            c0 = 8 * par

            def v_apply(eng):
                return eng.tensor_scalar(dst_bf16, xsrc, st_(c0 + NMEAN),
                                         st_(c0 + R0), ALU.add, ALU.mult)
            return PR.add("vector", v_apply,
                          waits=[(s_ss, ss)] + list(apply_waits), inc=s_vln)

        def layernorm(par, xsrc, dst_bf16, sq_scr, stat_waits, apply_waits):
            ss = ln_stat(par, xsrc, sq_scr, stat_waits)
            return ln_apply(par, xsrc, dst_bf16, ss, apply_waits)

        twin_free = [(s_vp, 0), (s_vp, 0)]
        twin_i = [0]

        def transpose_pack(srcs, dst, dst_is_3d=False, extra_waits=(),
                           evac_eng="vector"):
            w = twin_i[0] % 2
            twin_i[0] += 1
            win = twb[w]
            waits = [twin_free[w]] + list(extra_waits)

            def t_fn(eng, srcs=srcs, win=win):
                inst = None
                for t, s in enumerate(srcs):
                    inst = eng.transpose(win[:, t * P:(t + 1) * P], s,
                                         id_sb[:, 0:P])
                return inst
            tv = PR.add("tensor", t_fn, waits=waits, inc=s_t)

            def v_fn(eng, dst=dst, win=win, n=len(srcs)):
                src = win[:, 0:n * P]
                if dst_is_3d:
                    src = src.rearrange("p (c x) -> p c x", c=n)
                if evac_eng == "vector":
                    return eng.tensor_copy(dst, src)
                return eng.activation(dst, src, AF.Identity,
                                      bias=0.0, scale=1.0)
            sem = s_vp if evac_eng == "vector" else s_sp2
            ev = PR.add(evac_eng, v_fn, waits=[(s_t, tv)], inc=sem)
            twin_free[w] = (sem, ev)
            return tv, ev

        ln_done = {}
        tpose_done = {}
        sp_evac = {}
        ss_of = {}
        fmm_evac = {}
        NGRP = NH * NHH

        def strip_stat(st):
            w = [(s_vln, ln_done[st - 4])] if st >= 4 else []
            xl = x_load(st, w, nbuf=4)
            ss_of[st] = ln_stat(st % 2, xb4[st % 4][:, :], sqscr[:, 0:D],
                                [xl])

        h1_win_evac = {}

        def strip_apply(st):
            if 0 < st <= 6:
                emit_warm(14 if st <= 4 else 10)
            half = st // NSH
            sl = st % NSH
            hh = sl // SPW
            pos = sl % SPW
            aw = [(s_t, tpose_done[st - 2])] if st >= 2 else []
            ln_done[st] = ln_apply(st % 2, xb4[st % 4][:, :],
                                   lnb[st % 2][:, :], ss_of[st], aw)
            srcs = [lnb[st % 2][:, j * P:(j + 1) * P] for j in range(NH)]
            if half == 0:
                dst = ln1T0.rearrange("p (c x) -> p c x", c=NH)[
                    :, :, sl * P:(sl + 1) * P]
            else:
                slot = hh % 2
                dst = ln1Tr[:, slot * NH * CHW:(slot + 1) * NH * CHW
                            ].rearrange("p (c x) -> p c x", c=NH)[
                    :, :, pos * P:(pos + 1) * P]
            ew = [(s_vln, ln_done[st])]
            if st == 0:
                ew.append((s_vq, id_done))
            if half == 1:
                if hh == 0 and pos == 0 and S >= 4096:
                    ew.append((s_ss, wprep["cast_last"]))
                if hh >= 2 and pos == 0:
                    ew.append((s_ve, h1_win_evac[hh - 2]))
            transpose_pack(srcs, dst, dst_is_3d=True, extra_waits=ew)
            tpose_done[st] = s_t.v
            sp_evac[st] = s_vp.v

        fmm_seq = [0]
        f_h0_ve = [0]
        pair_last = {0: 0, 1: 0, 2: 0, 3: 0, 4: 0}

        def emit_fmm_group(half, gi, lnT_ready, pair_override=None):
            hc = gi // NHH
            hh = gi % NHH
            gkey = fmm_seq[0]
            fmm_seq[0] += 1
            if pair_override is not None:
                pair = pair_override
            else:
                pair = gkey % 4
            mmw = [(s_vp, lnT_ready), (s_ss, wprep['cast_last'])]
            if pair == 4 and pair_last[4] == 0:
                mmw.append((s_vq, wprep['quant']))
            if pair_last[pair] > 0:
                mmw.append((s_ve, pair_last[pair]))
            nnb = max(CHW // 512, 1)
            kw = min(512, CHW)

            def t_fmm(eng, hc=hc, hh=hh, pair=pair, half=half):
                inst = None
                for dc in range(NH):
                    for nb in range(nnb):
                        if half == 0:
                            base = dc * HALF + hh * CHW
                            lnt = ln1T0
                        else:
                            base = (hh % 2) * NH * CHW + dc * CHW
                            lnt = ln1Tr
                        inst = eng.matmul(
                            pb[pair + nb][:, 0:kw],
                            WT[:, dc * D + hc * P:dc * D + (hc + 1) * P],
                            lnt[:, base + nb * kw:base + (nb + 1) * kw],
                            start=(dc == 0), stop=(dc == NH - 1))
                return inst
            gdone = PR.add("tensor", t_fmm, waits=mmw, inc=s_t)

            def s_fevac(eng, hc=hc, hh=hh, pair=pair, half=half):
                inst = None
                for nb in range(nnb):
                    base = hc * S + half * HALF + hh * CHW + nb * kw
                    inst = eng.activation(
                        fT[:, base:base + kw], pb[pair + nb][:, 0:kw],
                        AF.Identity, bias=b_sb[:, hc:hc + 1], scale=1.0)
                return inst
            PR.add("scalar", s_fevac,
                   waits=[(s_t, gdone), (dsem_b, b_loaded)], inc=s_ve)
            fmm_evac[gkey] = s_ve.v
            pair_last[pair] = s_ve.v
            if half == 0:
                f_h0_ve[0] = s_ve.v
            else:
                h1_win_evac[hh] = s_ve.v

        SPW = NSH // NHH
        fmm_sched = {}
        fmm_tail = []
        for w in range(2 * NHH):
            h, hh = divmod(w, NHH)
            wlast = h * NSH + hh * SPW + (SPW - 1)
            for k, hc in enumerate(range(NH)):
                gi = hc * NHH + hh
                target = (w + 1) * SPW + k // 2
                if target < NST:
                    fmm_sched.setdefault(target, []).append((h, gi, wlast))
                else:
                    fmm_tail.append((h, gi, wlast))

        def run_sched(st):
            for (h, gi, wlast) in fmm_sched.get(st, []):
                emit_fmm_group(h, gi, sp_evac[wlast])

        prep_sched = {
            0: [lambda c=c: emit_cast_c(c) for c in range(4)],
            1: [lambda c=c: emit_cast_c(c) for c in range(4, NH)],
            2: [lambda c=c: emit_ft_issue(c) for c in range(6)],
            3: [lambda: emit_ft_issue(6), lambda: emit_ft_issue(7)],
        }
        def flush_prep(sa):
            for k in sorted(k for k in prep_sched if k <= sa):
                for th in prep_sched.pop(k):
                    th()

        for c in (0, 2, 4, 6):
            emit_wt_issue(c)
        for c in (1, 3, 5, 7):
            emit_wt_issue(c)
        strip_stat(0)
        strip_stat(1)
        strip_apply(0)
        flush_prep(0)
        run_sched(0)
        if NST > 2:
            strip_stat(2)
        strip_apply(1)
        run_sched(1)
        flush_prep(1)
        for st in range(3, NST + 1):
            if st < NST:
                strip_stat(st)
            strip_apply(st - 1)
            flush_prep(st - 1 if st < NST else 10 ** 9)
            run_sched(st - 1)

        def emit_1c(stt, first_wait=None, evac_eng=None):
            srcs = [fT[:, j * S + stt * P:j * S + (stt + 1) * P]
                    for j in range(NH)]
            dst = f_sb[:, stt * D:stt * D + D]
            ew = [first_wait] if first_wait else []
            if evac_eng is None:
                evac_eng = "scalar" if stt < NSH else "vector"
            transpose_pack(srcs, dst, extra_waits=ew, evac_eng=evac_eng)

        c1_half0 = list(range(NSH))
        qrem = [0, 1, 2, 3, 6, 7, 4, 5]
        first0 = True

        def next_1c():
            nonlocal first0
            if c1_half0:
                stt = c1_half0.pop(0)
                emit_1c(stt, ((s_ve, f_h0_ve[0]) if first0 else None))
                first0 = False

        for (h, gi, wlast) in [t for t in fmm_tail if t[0] == 0]:
            emit_fmm_group(h, gi, sp_evac[wlast])
        for g in range(4):
            emit_absum_g(g)
        nall = 0
        for i, (h, gi, wlast) in enumerate(
                [t for t in fmm_tail if t[0] == 1]):
            emit_fmm_group(h, gi, sp_evac[wlast])
            if 1 <= i <= 4:
                emit_allred_g(i - 1, first=(i == 1), last=(i == 4))
                nall = i
            next_1c()
        while nall < 4:
            emit_allred_g(nall, first=(nall == 0), last=(nall == 3))
            nall += 1
        f_ready_ve = s_ve.v
        emit_qpre()
        for c in (0, 1, 2, 3, 6, 7):
            emit_quant_c(c)

        e_evac = {}
        e_mm_stop = {}
        exp_done = {}
        max_done = {}
        a_done = {}
        out1_done = {}
        ln2_done = {}
        ln2T_evd = {}
        ffn_t_done = {}
        ffn_done = {}
        pT_evac = {}
        A_done = {}

        def emit_e_mms(s, pre_sweep=None, h_waits=None):
            for h in range(2):
                for kb in range(NKB):
                    e_evac[(s, h, kb)] = s_ve.v + h * NKB + kb + 1
            for h in range(2):
                for kb in range(NKB):
                    if pre_sweep and (h * NKB + kb) in pre_sweep:
                        pre_sweep[h * NKB + kb]()
                    for hc in range(NH):
                        wts = []
                        if hc == 0:
                            if h_waits is not None:
                                wts.append((s_ve, h_waits[h]))
                            if h == 1:
                                wts.append((s_ve, e_evac[(s, 0, kb)]))
                            elif s > 0:
                                wts.append((s_ve, e_evac[(s - 1, 1, kb)]))

                        def t_e(eng, s=s, h=h, hc=hc, kb=kb):
                            return eng.matmul(
                                pb[kb][:, 0:KW],
                                fT[:, hc * S + s * P:hc * S + (s + 1) * P],
                                fT[:, hc * S + h * HALF + kb * KW:
                                   hc * S + h * HALF + (kb + 1) * KW],
                                start=(hc == 0), stop=(hc == NH - 1))
                        v = PR.add("tensor", t_e, waits=wts,
                                   inc=s_t if hc == NH - 1 else None)
                        if hc == NH - 1:
                            e_mm_stop[(s, h, kb)] = v

        def emit_e_evacs(s):
            c0 = max(s * P - (MAXD - 1), 0)
            c1 = min(s * P + P + MAXD - 1, S)
            for h in range(2):
                for kb in range(NKB):
                    bi = h * NKB + kb
                    lo = h * HALF + kb * KW
                    hi = lo + KW
                    bc0 = max(lo, c0)
                    bc1 = min(hi, c1)
                    wts = [(s_t, e_mm_stop[(s, h, kb)])]
                    if s > 0 and h == 0 and kb == 0:
                        wts.append((s_sx, exp_done[s - 1]))
                    if s == 0 and h == 0 and kb == 0:
                        wts.append((s_vq, wprep["quant_e"]))

                    def s_ec(eng, kb=kb, lo=lo, hi=hi):
                        return eng.copy(e_sb[:, lo:hi], pb[kb][:, 0:KW])
                    got = PR.add("scalar", s_ec, waits=wts, inc=s_ve)
                    assert got == e_evac[(s, h, kb)], "evac reservation"

                    def v_pm(eng, s=s, lo=lo, hi=hi, bc0=bc0, bc1=bc1,
                             bi=bi):
                        if bc1 > bc0:
                            u0 = bc0 - (s * P - (MAXD - 1))
                            eng.tensor_tensor(
                                e_sb[:, bc0:bc1], e_sb[:, bc0:bc1],
                                band[:, u0:u0 + (bc1 - bc0)], ALU.add)
                        return eng.reduce_max(st_(PM0 + bi), e_sb[:, lo:hi],
                                              AXX)
                    PR.add("vector", v_pm,
                           waits=[(s_ve, e_evac[(s, h, kb)])], inc=s_vmax)

            def v_max(eng, s=s):
                return eng.reduce_max(st_(NMAX0 + s % 2),
                                      stats[:, PM0:PM0 + 2 * NKB], AXX,
                                      negate=True)
            max_done[s] = PR.add("vector", v_max, inc=s_vmax)

        def emit_exp(s):
            sc_w = [(s_vmax, max_done[s])]
            if s == 0:
                sc_w.append((s_vq, wprep["quant_p"]))
            if s > 0:
                sc_w.append((s_t, A_done[s - 1]))

            def s_exp(eng, s=s):
                return eng.activation(p_sb[:, 0:S], e_sb[:, 0:S], AF.Exp,
                                      bias=st_(NMAX0 + s % 2), scale=1.0,
                                      accum_out=st_(LSUM0 + s % 2))
            exp_done[s] = PR.add("scalar", s_exp, waits=sc_w, inc=s_sx)

        e0_h0 = max(pair_last[0], pair_last[1], pair_last[2], pair_last[3],
                    f_h0_ve[0])
        emit_e_mms(0, h_waits=[e0_h0, f_ready_ve])
        emit_e_evacs(0)
        emit_exp(0)
        for c in (4, 5):
            emit_quant_c(c)

        all_1c = c1_half0 + list(range(NSH, NST))
        onec_state = [True, 0]

        def pop_1c(n, eng=None):
            for _ in range(n):
                if not all_1c:
                    return
                stt = all_1c.pop(0)
                fw = None
                if stt >= NSH and onec_state[0]:
                    fw = (s_ve, f_ready_ve)
                    onec_state[0] = False
                i = onec_state[1]
                onec_state[1] += 1
                emit_1c(stt, fw, evac_eng=eng or
                        ("scalar" if i % 2 == 0 else "vector"))

        pop_1c(2)
        f_done = [(s_vp, s_vp.v), (s_sp2, s_sp2.v)]

        NJT = NST
        npack = (NJT + 7) // 8
        ss_ln2 = {}
        d_packs = {}
        lnp = {}

        for s in range(NST + 1):
            live = s < NST
            def emit_A_pack(s, g):
                j0 = 8 * g
                n = min(8, NJT - j0)
                srcs = [p_sb[:, (j0 + j) * P:(j0 + j + 1) * P]
                        for j in range(n)]
                dst = pT_sb[:, j0 * P:(j0 + n) * P]
                transpose_pack(srcs, dst,
                               extra_waits=[(s_sx, exp_done[s])])
                pT_evac[(s, g)] = s_vp.v

            if s >= 1 and s < NST:
                ss_ln2[s - 1] = ln_stat((s - 1) % 2, out1[(s - 1) % 2][:, :],
                                        sqscr[:, 0:D], [])
            elif s == NST:
                c0 = 8 * ((s - 1) % 2)

                def v_comb(eng, c0=c0):
                    eng.tensor_tensor(st_(c0 + SSUM), st_(ABS8),
                                      st_(ABS8 + 1), ALU.add)
                    eng.tensor_scalar_mul(st_(c0 + NMEAN),
                                          st_(c0 + SSUM), -1.0 / D)
                    eng.tensor_tensor(st_(c0 + VARSUM), st_(ABS8 + 2),
                                      st_(ABS8 + 3), ALU.add)
                    eng.tensor_tensor(st_(c0 + VP), st_(c0 + NMEAN),
                                      st_(c0 + NMEAN), ALU.mult)
                    return eng.tensor_scalar(st_(c0 + T1), st_(c0 + VP),
                                             -1.0, EPS, ALU.mult, ALU.add)
                vc = PR.add("vector", v_comb,
                            waits=[(s_ss, lnp["sq"])], inc=s_vm)

                def s_comb(eng, c0=c0):
                    eng.activation(st_(c0 + STD), st_(c0 + VARSUM),
                                   AF.Ln, bias=st_(c0 + T1), scale=1.0 / D)
                    return eng.activation(st_(c0 + R0), st_(c0 + STD),
                                          AF.Exp, bias=0.0, scale=-0.5)
                ss_ln2[s - 1] = PR.add("scalar", s_comb,
                                       waits=[(s_vm, vc)], inc=s_ss)

            if live and s + 1 < NST:
                if s > 0:
                    pre = {k + 1: (lambda s=s, g=k: emit_A_pack(s, g))
                           for k in range(npack)}
                else:
                    pre = None
                emit_e_mms(s + 1, pre_sweep=pre)
                if s == 0:
                    pop_1c(99, eng="vector")
                    f_done[:] = [(s_vp, s_vp.v), (s_sp2, s_sp2.v)]
                    for g in range(npack):
                        emit_A_pack(s, g)
                A_done[s] = s_t.v
            elif live:
                emit_warm(12, bank=0,
                          waits=[(s_ve, e_evac[(NST - 1, 1, NKB - 1)])])
                for g in range(npack):
                    emit_A_pack(s, g)
                A_done[s] = s_t.v

            if s >= 1:
                if s >= NST - 1:
                    ww = [(s_ve, e_evac[(NST - 1, 1, NKB - 1)])]
                    if s == NST:
                        emit_warm(10, bank=1,
                                  waits=ww + [(s_vo, out1_done[NST - 1])])
                        ww = ww + [(s_ss, ss_ln2[s - 1])]
                    emit_warm(14 if s == NST - 1 else 20, bank=0, waits=ww)
                aw = [(s_t, d_packs[s - 2])] if s >= 2 else []
                ln2_done[s - 1] = ln_apply((s - 1) % 2,
                                           out1[(s - 1) % 2][:, :],
                                           ln2[:, :], ss_ln2[s - 1], aw)
                srcs = [ln2[:, j * P:(j + 1) * P] for j in range(NH)]
                transpose_pack(srcs, ln2T[:, 0:D],
                               extra_waits=[(s_vln, ln2_done[s - 1])])
                d_packs[s - 1] = s_t.v
                ln2T_evd[s - 1] = s_vp.v

            if live and s + 1 < NST:
                emit_e_evacs(s + 1)
                emit_exp(s + 1)

            if live:
                a_done_nb = {}
                for nb in range(NBD):
                    for jt in range(NJT):
                        wts = []
                        if nb == 0 and jt == 0:
                            wts.append((s_vp, pT_evac[(s, 0)]))
                            wts.extend(f_done)
                            wts.append((s_vq, wprep['quant']))
                            if s == 0:
                                wts.append((s_ve, f_ready_ve))
                            if s > 0:
                                wts.append((s_vo, out1_done[s - 1]))
                            if s > 1:
                                wts.append((s_vf, ffn_done[s - 2]))
                        elif nb == 0 and jt % 8 == 0:
                            wts.append((s_vp, pT_evac[(s, jt // 8)]))

                        def t_a(eng, s=s, jt=jt, nb=nb):
                            return eng.matmul(
                                fw_ps[nb],
                                pT_sb[:, jt * P:(jt + 1) * P],
                                f_sb[:, jt * D + nb * 512:
                                     jt * D + (nb + 1) * 512],
                                start=(jt == 0), stop=(jt == NJT - 1))
                        PR.add("tensor", t_a, waits=wts,
                               inc=s_t if jt == NJT - 1 else None)
                    a_done_nb[nb] = s_t.v

                if s == 0:
                    xw = [(s_vln, ln_done[NST - 2])]
                elif s == 1:
                    xw = [(s_vln, ln_done[NST - 1])]
                else:
                    xw = [(s_vo, out1_done[s - 2])]
                xl = x_load(NST + s, xw)

                out1_nb = {}
                for nb in range(NBD):
                    def v_o1(eng, s=s, nb=nb):
                        if nb == 0:
                            eng.reciprocal(st_(RL), st_(LSUM0 + s % 2))
                        return eng.scalar_tensor_tensor(
                            out1[s % 2][:, nb * 512:(nb + 1) * 512],
                            fw_ps[nb], st_(RL),
                            xb[s % 2][:, nb * 512:(nb + 1) * 512],
                            ALU.mult, ALU.add)
                    w = [(s_t, a_done_nb[nb])]
                    if nb == 0:
                        w += [xl, (s_sx, exp_done[s])]
                    out1_nb[nb] = PR.add("vector", v_o1, waits=w, inc=s_vo)
                    if s == NST - 1:
                        def v_pp(eng, s=s, nb=nb):
                            return eng.reduce_sum(
                                st_(ABS8 + nb),
                                out1[s % 2][:, nb * 512:(nb + 1) * 512],
                                AXX)
                        PR.add("vector", v_pp, inc=s_vm)

                        def s_pp(eng, s=s, nb=nb):
                            return eng.activation(
                                sqscr[:, 0:512],
                                out1[s % 2][:, nb * 512:(nb + 1) * 512],
                                AF.Square, bias=0.0, scale=1.0,
                                accum_out=st_(ABS8 + 2 + nb))
                        lnp["sq"] = PR.add("scalar", s_pp,
                                           waits=[(s_vo, out1_nb[nb])],
                                           inc=s_ss)
                out1_done[s] = out1_nb[NBD - 1]

            if s >= 1:
                sm1 = s - 1

                wts_nb = {}
                for nb in range(NBD):
                    if live:
                        ow = out1_nb[nb]
                    else:
                        ow = out1_done[NST - 1]
                    wts = [(s_vo, ow)]
                    if nb == 0:
                        wts += [(s_vp, ln2T_evd[sm1]),
                                (s_vq, wprep['quant_last'])]
                    wts_nb[nb] = wts

                ffn_t_nb = {}
                for nb in range(NBD):
                    def t_ffn(eng, sm1=sm1, nb=nb):
                        inst = None
                        for ic in range(NH):
                            inst = eng.matmul(
                                fw_ps[nb],
                                ln2T[:, ic * P:(ic + 1) * P],
                                wqT[:, ic * D + nb * 512:
                                    ic * D + (nb + 1) * 512],
                                start=(ic == 0), stop=(ic == NH - 1))
                        return inst
                    ffn_t_nb[nb] = PR.add("tensor", t_ffn,
                                          waits=wts_nb[nb], inc=s_t)
                ffn_t_done[sm1] = ffn_t_nb[NBD - 1]

                if sm1 == NST - 1:
                    for nb in range(NBD):
                        fwc = [(s_t, ffn_t_nb[nb])]
                        if nb == 0:
                            fwc.append((dsem_o, dsem_o.v))

                        def v_fc(eng, sm1=sm1, nb=nb):
                            return eng.scalar_tensor_tensor(
                                outb[:, nb * 512:(nb + 1) * 512],
                                fw_ps[nb], st_(SCALE),
                                out1[sm1 % 2][:, nb * 512:(nb + 1) * 512],
                                ALU.mult, ALU.add)
                        fd = PR.add("vector", v_fc, waits=fwc, inc=s_vf)

                        def y_c(eng, sm1=sm1, nb=nb):
                            return eng.dma_start(
                                out_d[sm1 * P:(sm1 + 1) * P,
                                      nb * 512:(nb + 1) * 512],
                                outb[:, nb * 512:(nb + 1) * 512])
                        PR.add("sync", y_c, waits=[(s_vf, fd)],
                               inc=dsem_o, inc_amt=16)
                    ffn_done[sm1] = fd
                else:
                    fw = [(s_t, ffn_t_done[sm1]), (dsem_o, dsem_o.v)]

                    def v_final(eng, sm1=sm1):
                        inst = None
                        for nb in range(NBD):
                            inst = eng.scalar_tensor_tensor(
                                outb[:, nb * 512:(nb + 1) * 512],
                                fw_ps[nb], st_(SCALE),
                                out1[sm1 % 2][:, nb * 512:(nb + 1) * 512],
                                ALU.mult, ALU.add)
                        return inst
                    ffn_done[sm1] = PR.add("vector", v_final, waits=fw,
                                           inc=s_vf)

                    def y_out(eng, sm1=sm1):
                        return eng.dma_start(
                            out_d[sm1 * P:(sm1 + 1) * P, :], outb[:, :])
                    PR.add("sync", y_out, waits=[(s_vf, ffn_done[sm1])],
                           inc=dsem_o, inc_amt=16)

        PR.raw_wait("sync", dsem_o, dsem_o.v)

        PR.check()

        with nc.Block() as block:
            @block.tensor
            def _(eng):
                PR.emit(eng, "tensor", gsems.get("tensor"))

            @block.vector
            def _(eng):
                PR.emit(eng, "vector", gsems.get("vector"))

            @block.scalar
            def _(eng):
                PR.emit(eng, "scalar", gsems.get("scalar"))

            @block.sync
            def _(eng):
                PR.emit(eng, "sync", gsems.get("sync"))

            @block.gpsimd
            def _(eng):
                PR.emit(eng, "gpsimd", gsems.get("gpsimd"))

    return nc


_NC_CACHE = {}


def get_nc(S=4096):
    if S not in _NC_CACHE:
        _NC_CACHE[S] = build_nc(S)
    return _NC_CACHE[S]


def kernel(L0_attn_x, L0_attn_W, L0_attn_b, L0_ffn_weight):
    x = np.ascontiguousarray(np.asarray(L0_attn_x))
    W = np.asarray(L0_attn_W)
    b = np.ascontiguousarray(np.asarray(L0_attn_b))
    wf = np.asarray(L0_ffn_weight)
    wt = np.ascontiguousarray(W.T)
    ft = np.ascontiguousarray(wf.T)

    B, S, Dm = x.shape
    assert Dm == D and B == 8
    nc = get_nc(S)
    in_maps = [{"x": x[c], "wt": wt, "b": b, "ft": ft} for c in range(B)]
    res = run_bass_kernel_spmd(nc, in_maps, core_ids=list(range(B)))
    return np.stack([np.asarray(r["out"]) for r in res.results], axis=0)


# version 44
# speedup vs baseline: 1.0009x; 1.0009x over previous
import contextlib
import numpy as np
import concourse.bass as bass
import concourse.mybir as mybir
from concourse.bass_utils import run_bass_kernel_spmd

F32 = mybir.dt.float32
BF16 = mybir.dt.bfloat16
AF = mybir.ActivationFunctionType
ALU = mybir.AluOpType
AXX = mybir.AxisListType.X

P = 128
D = 1024
NH = D // P
NBD = D // 512
EPS = 1e-5
MAXD = 10
BSCALE = 0.01
BW = P + 2 * MAXD - 2
KiB = 1024


class SemCtr:
    def __init__(self, h, name):
        self.h = h
        self.name = name
        self.v = 0

    def bump(self, amt=1):
        self.v += amt
        return self.v


class Prog:
    ENGINES = ("tensor", "vector", "scalar", "sync", "gpsimd")

    def __init__(self):
        self.streams = {e: [] for e in self.ENGINES}
        self.all_waits = []

    def add(self, eng, fn, waits=(), inc=None, inc_amt=1):
        waits = [(s, v) for (s, v) in waits if v > 0]
        self.all_waits.extend(waits)
        post = inc.bump(inc_amt) if inc is not None else None
        self.streams[eng].append((fn, waits, inc, inc_amt, post))
        return post

    def raw_wait(self, eng, sem, val):
        self.streams[eng].append((lambda e: None, [(sem, val)], None, 1, None))

    def emit(self, eng_obj, name, guard_sem=None):
        guard = None
        if guard_sem is not None:
            guard = _ChainGuard(eng_obj, guard_sem)
            eng_obj = guard
        waited = {}
        for fn, waits, inc, inc_amt, post in self.streams[name]:
            for (s, v) in waits:
                if waited.get(s.name, -1) >= v:
                    continue
                waited[s.name] = v
                eng_obj.wait_ge(s.h, v)
            inst = fn(eng_obj)
            if inc is not None:
                assert inst is not None
                if guard is not None and inc_amt != 16:
                    guard.commit_group(inst, inc, post)
                else:
                    inst.then_inc(inc.h, inc_amt)

    def check(self):
        for (s, v) in self.all_waits:
            assert v <= s.v, f"wait {s.name}>={v} exceeds final {s.v}"


class _ChainGuard:

    def __init__(self, eng, sem):
        self._eng = eng
        self._sem = sem
        self._n = 0
        self._prev = None
        self._marker = None

    def _flush_prev(self):
        if self._prev is not None:
            self._prev.then_inc(self._sem.h, 1)
            self._n += 1
            self._marker = (self._sem.h, self._n)
            self._prev = None

    def commit_group(self, inst, inc, post_val):
        assert inst is self._prev or self._prev is None
        self._prev = None
        inst.then_inc(inc.h, 1)
        self._marker = (inc.h, post_val)

    def wait_ge(self, *a, **k):
        return self._eng.wait_ge(*a, **k)

    def dma_start(self, *a, **k):
        return self._eng.dma_start(*a, **k)

    def __getattr__(self, name):
        real = getattr(self._eng, name)
        if not callable(real):
            return real

        def wrap(*a, **k):
            self._flush_prev()
            if self._marker is not None:
                self._eng.wait_ge(self._marker[0], self._marker[1])
            inst = real(*a, **k)
            self._prev = inst
            return inst
        return wrap


def build_nc(S, sim_mode=False):
    NST = S // P
    HALF = S // 2
    NSH = NST // 2
    KW = min(512, HALF)
    NKB = HALF // KW
    assert NKB <= 4
    CHW = min(512, HALF)
    NHH = HALF // CHW
    SPW = NSH // NHH

    nc = bass.Bass("TRN2", target_bir_lowering=False,
                   detect_race_conditions=sim_mode)

    x_d = nc.dram_tensor("x", [S, D], F32, kind="ExternalInput").ap()
    wt_d = nc.dram_tensor("wt", [D, D], F32, kind="ExternalInput").ap()
    b_d = nc.dram_tensor("b", [D], F32, kind="ExternalInput").ap()
    ft_d = nc.dram_tensor("ft", [D, D], F32, kind="ExternalInput").ap()
    out_d = nc.dram_tensor("out", [S, D], F32, kind="ExternalOutput").ap()

    off = {}
    cur = 0

    def place(name, nbytes, align=512):
        nonlocal cur
        cur = (cur + align - 1) // align * align
        off[name] = cur
        cur += nbytes

    place("fT", NH * S * 2)
    f_base = cur
    place("f", NST * D * 2)
    e_base = cur
    place("e", S * 4)
    pT_base = cur
    place("pT", S * 2)
    place("p", S * 2)
    if S >= 4096:
        off["wstage"] = e_base
        off["WT"] = f_base + 32 * KiB
        off["ln1T0"] = f_base
        off["wtstage"] = f_base + 48 * KiB
        off["ln1Tr"] = f_base + 48 * KiB
        assert NH * HALF * 2 <= 32 * KiB and 64 * KiB <= NST * D * 2
        assert 2 * NH * CHW * 2 <= 16 * KiB
    else:
        place("wstage", NH * D * 4)
        place("WT", NH * D * 2)
        place("wtstage", 4 * D * 4)
        place("ln1T0", NH * HALF * 2)
        place("ln1Tr", 2 * NH * CHW * 2)
    place("xbuf0", D * 4)
    place("xbuf1", D * 4)
    lnbuf_base = cur
    place("lnbuf0", D * 2)
    place("lnbuf1", D * 2)
    place("out1a", D * 4)
    place("outbuf", D * 4)
    place("ln2", D * 2)
    place("ln2T", D * 2)
    place("wqT", NH * D * 2)
    off["wtstage2"] = off["wqT"]
    assert 4 * D * 4 <= NH * D * 2
    place("b_sb", NH * 4)
    place("consts", 8 * 4)
    place("stats", 48 * 4)
    place("id", P * 2)
    place("band", BW * 4)
    place("iotascratch", BW * 4)
    place("ones", P * 4)
    place("sqscr", D * 2)
    off["out1b"] = lnbuf_base
    ARENA_BASE = 16512
    assert ARENA_BASE + cur <= 229344, f"arena overflow: {cur}"

    with contextlib.ExitStack() as ctx:
        pb = [ctx.enter_context(nc.psum_tensor(f"pb{i}", [P, 512], F32))
              for i in range(6)]
        pbt = [ctx.enter_context(nc.psum_tensor(f"pbt{i}", [P, 1024], BF16))
               for i in range(2)]

        def view(name, cols, dt):
            h = nc.alloc_sbuf_tensor_at(name, [P, cols], dt,
                                        offset=ARENA_BASE + off[name])
            return h[:, :]

        fT = view("fT", NH * S, BF16)
        f_sb = view("f", NST * D, BF16)
        wstage = view("wstage", NH * D, F32)
        wtstage = view("wtstage", 4 * D, F32)
        wtstage2 = view("wtstage2", 4 * D, F32)
        WT = view("WT", NH * D, BF16)
        ln1T0 = view("ln1T0", NH * HALF, BF16)
        ln1Tr = view("ln1Tr", 2 * NH * CHW, BF16)
        e_sb = view("e", S, F32)
        pT_sb = view("pT", S, BF16)
        p_sb = view("p", S, BF16)
        xb = [view("xbuf0", D, F32), view("xbuf1", D, F32)]
        lnb = [view("lnbuf0", D, BF16), view("lnbuf1", D, BF16)]
        out1 = [view("out1a", D, F32), view("out1b", D, F32)]
        outb = view("outbuf", D, F32)
        xb4 = xb + [out1[0], outb]
        sqscr = view("sqscr", D, BF16)
        ln2 = view("ln2", D, BF16)
        ln2T = view("ln2T", D, BF16)
        wqT = view("wqT", NH * D, BF16)
        b_sb = view("b_sb", NH, F32)
        consts = view("consts", 8, F32)
        stats = view("stats", 48, F32)
        id_sb = view("id", P, BF16)
        band = view("band", BW, F32)
        iosc = view("iotascratch", BW, F32)
        ones_sb = view("ones", P, F32)

        SSUM, NMEAN, VARSUM, STD, VP, R0, T1, RSTD = range(8)
        LSUM0, LSUM1, RL, SCALE, HTH, NHTH, ABSP, TOT = range(16, 24)
        PM0 = 24
        NMAX0, NMAX1, ROWI, ROWB = 32, 33, 34, 35
        ABS8 = 40

        def st_(c):
            return stats[:, c:c + 1]

        twb = [pbt[0][:, 0:1024], pbt[1][:, 0:1024]]
        fw_ps = [pb[4][:, 0:512], pb[5][:, 0:512]]

        def mksem(name):
            return SemCtr(ctx.enter_context(nc.semaphore(name)), name)

        dsem_x = [mksem("dx0"), mksem("dx1"), mksem("dx2"), mksem("dx3")]
        dsem_xg = [mksem("dxg1"), mksem("dxg3")]
        dsem_ft = [mksem(f"dft{i}") for i in range(4)]
        dsem_b = mksem("dbb")
        dsem_w = [mksem(f"dw{i}") for i in range(8)]
        dsem_o = mksem("do")
        s_t = mksem("st")
        s_vp = mksem("svp")
        s_ve = mksem("sve")
        s_vo = mksem("svo")
        s_vm = mksem("svm")
        s_vln = mksem("svln")
        s_vmax = mksem("svmax")
        s_vf = mksem("svf")
        s_ss = mksem("sss")
        s_sx = mksem("ssx")
        s_gp = mksem("sgp")
        s_vq = mksem("svq")
        s_sp2 = mksem("ssp2")

        gsems = {}
        for e in ("vector", "scalar", "gpsimd", "sync"):
            gsems[e] = mksem("guard_" + e)
        if sim_mode:
            gsems["tensor"] = mksem("guard_tensor")

        PR = Prog()

        CONST_VALS = [0.0, 1.0, EPS, 1.0 / D, -0.5]
        for ci, cv in enumerate(CONST_VALS):
            nc.const_aps.register(F32, cv, consts[:, ci:ci + 1])

        def v_consts(eng):
            for ci, cv in enumerate(CONST_VALS):
                eng.memset(consts[:, ci:ci + 1], cv)
            return eng.memset(ones_sb[:, 0:P], 1.0)
        PR.add("vector", v_consts, inc=s_vq)

        def g_iota(eng):
            eng.iota(iosc[:, 0:P], [[1, P]], channel_multiplier=0,
                     allow_small_or_imprecise_dtypes=True)
            eng.iota(st_(ROWI), [[1, 1]], channel_multiplier=1,
                     allow_small_or_imprecise_dtypes=True)
            eng.iota(band[:, 0:BW], [[1, BW]], channel_multiplier=0,
                     allow_small_or_imprecise_dtypes=True)
            return eng.iota(st_(ROWB), [[1, 1]], base=MAXD - 1,
                            channel_multiplier=1,
                            allow_small_or_imprecise_dtypes=True)
        g_iota_done = PR.add("gpsimd", g_iota, inc=s_gp)

        def g_b(eng):
            return eng.dma_start(b_sb[:, 0:NH],
                                 b_d.rearrange("(c p) -> p c", p=P),
                                 allow_slow_non_contiguous=True)
        b_loaded = PR.add("gpsimd", g_b, inc=dsem_b, inc_amt=16)

        ftg_waits = {}

        def emit_ft_issue(c):
            def g_ft(eng, c=c):
                return eng.dma_start(wstage[:, c * D:(c + 1) * D],
                                     ft_d[c * P:(c + 1) * P, :])
            sem = dsem_ft[c // 2]
            PR.add("gpsimd" if c < 4 else "scalar", g_ft,
                   inc=sem, inc_amt=16)
            if c % 2 == 1:
                ftg_waits[c // 2] = (sem, sem.v)

        def v_id(eng):
            eng.tensor_scalar(id_sb[:, 0:P], iosc[:, 0:P], st_(ROWI), None,
                              ALU.is_equal)
            eng.tensor_scalar(band[:, 0:BW], band[:, 0:BW], st_(ROWB), None,
                              ALU.subtract)
            eng.tensor_scalar_add(iosc[:, 0:BW], band[:, 0:BW], float(MAXD))
            eng.tensor_scalar(band[:, 0:BW], band[:, 0:BW], -1.0,
                              float(MAXD), ALU.mult, ALU.add)
            eng.tensor_tensor(band[:, 0:BW], band[:, 0:BW], iosc[:, 0:BW],
                              ALU.min)
            eng.tensor_scalar_max(band[:, 0:BW], band[:, 0:BW], 0.0)
            return eng.tensor_scalar_mul(band[:, 0:BW], band[:, 0:BW],
                                         -BSCALE)
        id_done = PR.add("vector", v_id, waits=[(s_gp, g_iota_done)], inc=s_vq)

        def emit_warm(n, bank=5, waits=()):
            def t_warm(eng, n=n, bank=bank):
                inst = None
                for _ in range(n):
                    inst = eng.matmul(pb[bank][:, 0:P], ones_sb[:, 0:P],
                                      ones_sb[:, 0:P], start=True, stop=True)
                return inst
            PR.add("tensor", t_warm, waits=list(waits), inc=s_t)

        wprep = {}
        cast_ms = {}
        wt_lands = {}

        def wt_slot_ap(c):
            stg = wtstage if c < 4 else wtstage2
            sl = c % 4
            return stg[:, sl * D:(sl + 1) * D]

        def emit_wt_issue(c):
            sem = dsem_w[c]

            def fn(eng, c=c):
                return eng.dma_start(wt_slot_ap(c),
                                     wt_d[c * P:(c + 1) * P, :])
            PR.add("scalar" if c % 2 == 0 else "gpsimd", fn,
                   inc=sem, inc_amt=16)
            wt_lands[c] = (sem, sem.v)

        def emit_cast_c(c):
            def fn(eng, c=c):
                return eng.activation(WT[:, c * D:(c + 1) * D],
                                      wt_slot_ap(c),
                                      AF.Identity, bias=0.0, scale=1.0)
            cast_ms[c] = PR.add("scalar", fn, waits=[wt_lands[c]], inc=s_ss)
            if c == NH - 1:
                wprep["cast_last"] = cast_ms[c]

        absum_ms = {}

        def emit_absum_g(g):
            def v_absum(eng, g=g):
                inst = None
                for c in (2 * g, 2 * g + 1):
                    inst = eng.reduce_sum(st_(ABS8 + c),
                                          wstage[:, c * D:(c + 1) * D], AXX,
                                          apply_absolute_value=True)
                return inst
            absum_ms[g] = PR.add("vector", v_absum, waits=[ftg_waits[g]],
                                 inc=s_vq)

        def emit_allred_g(g, first=False, last=False):
            def t_allred(eng, g=g):
                return eng.matmul(pb[4][:, 0:2], ones_sb[:, 0:P],
                                  stats[:, ABS8 + 2 * g:ABS8 + 2 * g + 2],
                                  start=first, stop=last)
            done = PR.add("tensor", t_allred,
                          waits=[(s_vq, absum_ms[g])], inc=s_t)
            if last:
                wprep["allred"] = done

        def emit_qpre():
            def v_qpre(eng):
                eng.tensor_copy(stats[:, ABS8:ABS8 + 2], pb[4][:, 0:2])
                eng.reduce_sum(st_(TOT), stats[:, ABS8:ABS8 + 2], AXX)
                eng.tensor_scalar(st_(SCALE), st_(TOT), 1.0 / (D * D), EPS,
                                  ALU.mult, ALU.max)
                eng.tensor_scalar_mul(st_(HTH), st_(SCALE), 0.5)
                return eng.tensor_scalar_mul(st_(NHTH), st_(SCALE), -0.5)
            wprep["quant"] = PR.add("vector", v_qpre,
                                    waits=[(s_t, wprep["allred"])], inc=s_vq)

        def emit_quant_c(c):
            def v_q(eng, c=c):
                eng.tensor_single_scalar(wqT[:, c * D:(c + 1) * D],
                                         wstage[:, c * D:(c + 1) * D],
                                         st_(NHTH), ALU.is_lt)
                return eng.scalar_tensor_tensor(
                    wqT[:, c * D:(c + 1) * D],
                    wstage[:, c * D:(c + 1) * D], st_(HTH),
                    wqT[:, c * D:(c + 1) * D], ALU.is_gt, ALU.subtract)
            waits = [(s_ss, wprep["cast_last"])] \
                if not wprep.get("quant_first") else []
            done = PR.add("vector", v_q, waits=waits, inc=s_vq)
            wprep["quant_first"] = True
            if c == 3:
                wprep["quant_e"] = done
            if c == 7:
                wprep["quant_p"] = done
            if c == 5:
                wprep["quant_last"] = done

        def x_load(k, waits, nbuf=2):
            def fn(eng, k=k, nbuf=nbuf):
                st = k % NST
                buf = xb4[k % nbuf] if nbuf == 4 else xb[k % 2]
                return eng.dma_start(buf[:, :], x_d[st * P:(st + 1) * P, :])
            if nbuf == 4 and k % 2 == 1 and k >= 9:
                eng_name = "gpsimd"
                sem = dsem_xg[(k % 4) // 2]
            else:
                eng_name = "sync"
                sem = dsem_x[k % nbuf]
            PR.add(eng_name, fn, waits=waits, inc=sem, inc_amt=16)
            return (sem, sem.v)

        def ln_stat(par, xsrc, sq_scr, stat_waits):
            c0 = 8 * par

            def v_stat(eng):
                eng.reduce_sum(st_(c0 + SSUM), xsrc, AXX)
                return eng.tensor_scalar_mul(st_(c0 + NMEAN), st_(c0 + SSUM),
                                             -1.0 / D)
            vm = PR.add("vector", v_stat, waits=stat_waits, inc=s_vm)

            def s_var(eng):
                eng.activation(sq_scr, xsrc, AF.Square,
                               bias=st_(c0 + NMEAN), scale=1.0,
                               accum_out=st_(c0 + VARSUM))
                eng.activation(st_(c0 + STD), st_(c0 + VARSUM),
                               AF.Ln, bias=EPS, scale=1.0 / D)
                return eng.activation(st_(c0 + R0), st_(c0 + STD),
                                      AF.Exp, bias=0.0, scale=-0.5)
            return PR.add("scalar", s_var, waits=[(s_vm, vm)], inc=s_ss)

        def ln_apply(par, xsrc, dst_bf16, ss, apply_waits):
            c0 = 8 * par

            def v_apply(eng):
                return eng.tensor_scalar(dst_bf16, xsrc, st_(c0 + NMEAN),
                                         st_(c0 + R0), ALU.add, ALU.mult)
            return PR.add("vector", v_apply,
                          waits=[(s_ss, ss)] + list(apply_waits), inc=s_vln)

        def layernorm(par, xsrc, dst_bf16, sq_scr, stat_waits, apply_waits):
            ss = ln_stat(par, xsrc, sq_scr, stat_waits)
            return ln_apply(par, xsrc, dst_bf16, ss, apply_waits)

        twin_free = [(s_vp, 0), (s_vp, 0)]
        twin_i = [0]

        def transpose_pack(srcs, dst, dst_is_3d=False, extra_waits=(),
                           evac_eng="vector"):
            w = twin_i[0] % 2
            twin_i[0] += 1
            win = twb[w]
            waits = [twin_free[w]] + list(extra_waits)

            def t_fn(eng, srcs=srcs, win=win):
                inst = None
                for t, s in enumerate(srcs):
                    inst = eng.transpose(win[:, t * P:(t + 1) * P], s,
                                         id_sb[:, 0:P])
                return inst
            tv = PR.add("tensor", t_fn, waits=waits, inc=s_t)

            def v_fn(eng, dst=dst, win=win, n=len(srcs)):
                src = win[:, 0:n * P]
                if dst_is_3d:
                    src = src.rearrange("p (c x) -> p c x", c=n)
                if evac_eng == "vector":
                    return eng.tensor_copy(dst, src)
                return eng.activation(dst, src, AF.Identity,
                                      bias=0.0, scale=1.0)
            sem = s_vp if evac_eng == "vector" else s_sp2
            ev = PR.add(evac_eng, v_fn, waits=[(s_t, tv)], inc=sem)
            twin_free[w] = (sem, ev)
            return tv, ev

        ln_done = {}
        tpose_done = {}
        sp_evac = {}
        ss_of = {}
        fmm_evac = {}
        NGRP = NH * NHH

        def strip_stat(st):
            w = [(s_vln, ln_done[st - 4])] if st >= 4 else []
            xl = x_load(st, w, nbuf=4)
            ss_of[st] = ln_stat(st % 2, xb4[st % 4][:, :], sqscr[:, 0:D],
                                [xl])

        h1_win_evac = {}

        def strip_apply(st):
            if 0 < st <= 4:
                emit_warm(14)
            half = st // NSH
            sl = st % NSH
            hh = sl // SPW
            pos = sl % SPW
            aw = [(s_t, tpose_done[st - 2])] if st >= 2 else []
            ln_done[st] = ln_apply(st % 2, xb4[st % 4][:, :],
                                   lnb[st % 2][:, :], ss_of[st], aw)
            srcs = [lnb[st % 2][:, j * P:(j + 1) * P] for j in range(NH)]
            if half == 0:
                dst = ln1T0.rearrange("p (c x) -> p c x", c=NH)[
                    :, :, sl * P:(sl + 1) * P]
            else:
                slot = hh % 2
                dst = ln1Tr[:, slot * NH * CHW:(slot + 1) * NH * CHW
                            ].rearrange("p (c x) -> p c x", c=NH)[
                    :, :, pos * P:(pos + 1) * P]
            ew = [(s_vln, ln_done[st])]
            if st == 0:
                ew.append((s_vq, id_done))
            if half == 1:
                if hh == 0 and pos == 0 and S >= 4096:
                    ew.append((s_ss, wprep["cast_last"]))
                if hh >= 2 and pos == 0:
                    ew.append((s_ve, h1_win_evac[hh - 2]))
            transpose_pack(srcs, dst, dst_is_3d=True, extra_waits=ew)
            tpose_done[st] = s_t.v
            sp_evac[st] = s_vp.v

        fmm_seq = [0]
        f_h0_ve = [0]
        pair_last = {0: 0, 1: 0, 2: 0, 3: 0, 4: 0}

        def emit_fmm_group(half, gi, lnT_ready, pair_override=None):
            hc = gi // NHH
            hh = gi % NHH
            gkey = fmm_seq[0]
            fmm_seq[0] += 1
            if pair_override is not None:
                pair = pair_override
            else:
                pair = gkey % 4
            mmw = [(s_vp, lnT_ready), (s_ss, wprep['cast_last'])]
            if pair == 4 and pair_last[4] == 0:
                mmw.append((s_vq, wprep['quant']))
            if pair_last[pair] > 0:
                mmw.append((s_ve, pair_last[pair]))
            nnb = max(CHW // 512, 1)
            kw = min(512, CHW)

            def t_fmm(eng, hc=hc, hh=hh, pair=pair, half=half):
                inst = None
                for dc in range(NH):
                    for nb in range(nnb):
                        if half == 0:
                            base = dc * HALF + hh * CHW
                            lnt = ln1T0
                        else:
                            base = (hh % 2) * NH * CHW + dc * CHW
                            lnt = ln1Tr
                        inst = eng.matmul(
                            pb[pair + nb][:, 0:kw],
                            WT[:, dc * D + hc * P:dc * D + (hc + 1) * P],
                            lnt[:, base + nb * kw:base + (nb + 1) * kw],
                            start=(dc == 0), stop=(dc == NH - 1))
                return inst
            gdone = PR.add("tensor", t_fmm, waits=mmw, inc=s_t)

            def s_fevac(eng, hc=hc, hh=hh, pair=pair, half=half):
                inst = None
                for nb in range(nnb):
                    base = hc * S + half * HALF + hh * CHW + nb * kw
                    inst = eng.activation(
                        fT[:, base:base + kw], pb[pair + nb][:, 0:kw],
                        AF.Identity, bias=b_sb[:, hc:hc + 1], scale=1.0)
                return inst
            PR.add("scalar", s_fevac,
                   waits=[(s_t, gdone), (dsem_b, b_loaded)], inc=s_ve)
            fmm_evac[gkey] = s_ve.v
            pair_last[pair] = s_ve.v
            if half == 0:
                f_h0_ve[0] = s_ve.v
            else:
                h1_win_evac[hh] = s_ve.v

        SPW = NSH // NHH
        fmm_sched = {}
        fmm_tail = []
        for w in range(2 * NHH):
            h, hh = divmod(w, NHH)
            wlast = h * NSH + hh * SPW + (SPW - 1)
            for k, hc in enumerate(range(NH)):
                gi = hc * NHH + hh
                target = (w + 1) * SPW + k // 2
                if target < NST:
                    fmm_sched.setdefault(target, []).append((h, gi, wlast))
                else:
                    fmm_tail.append((h, gi, wlast))

        def run_sched(st):
            for (h, gi, wlast) in fmm_sched.get(st, []):
                emit_fmm_group(h, gi, sp_evac[wlast])

        prep_sched = {
            0: [lambda c=c: emit_cast_c(c) for c in range(4)],
            1: [lambda c=c: emit_cast_c(c) for c in range(4, NH)],
            2: [lambda c=c: emit_ft_issue(c) for c in range(6)],
            3: [lambda: emit_ft_issue(6), lambda: emit_ft_issue(7)],
        }
        def flush_prep(sa):
            for k in sorted(k for k in prep_sched if k <= sa):
                for th in prep_sched.pop(k):
                    th()

        for c in (0, 2, 4, 6):
            emit_wt_issue(c)
        for c in (1, 3, 5, 7):
            emit_wt_issue(c)
        strip_stat(0)
        strip_stat(1)
        strip_apply(0)
        flush_prep(0)
        run_sched(0)
        if NST > 2:
            strip_stat(2)
        strip_apply(1)
        run_sched(1)
        flush_prep(1)
        for st in range(3, NST + 1):
            if st < NST:
                strip_stat(st)
            strip_apply(st - 1)
            flush_prep(st - 1 if st < NST else 10 ** 9)
            run_sched(st - 1)

        def emit_1c(stt, first_wait=None, evac_eng=None):
            srcs = [fT[:, j * S + stt * P:j * S + (stt + 1) * P]
                    for j in range(NH)]
            dst = f_sb[:, stt * D:stt * D + D]
            ew = [first_wait] if first_wait else []
            if evac_eng is None:
                evac_eng = "scalar" if stt < NSH else "vector"
            transpose_pack(srcs, dst, extra_waits=ew, evac_eng=evac_eng)

        c1_half0 = list(range(NSH))
        qrem = [0, 1, 2, 3, 6, 7, 4, 5]
        first0 = True

        def next_1c():
            nonlocal first0
            if c1_half0:
                stt = c1_half0.pop(0)
                emit_1c(stt, ((s_ve, f_h0_ve[0]) if first0 else None))
                first0 = False

        for (h, gi, wlast) in [t for t in fmm_tail if t[0] == 0]:
            emit_fmm_group(h, gi, sp_evac[wlast])
        for g in range(4):
            emit_absum_g(g)
        nall = 0
        for i, (h, gi, wlast) in enumerate(
                [t for t in fmm_tail if t[0] == 1]):
            emit_fmm_group(h, gi, sp_evac[wlast])
            if 1 <= i <= 4:
                emit_allred_g(i - 1, first=(i == 1), last=(i == 4))
                nall = i
            next_1c()
        while nall < 4:
            emit_allred_g(nall, first=(nall == 0), last=(nall == 3))
            nall += 1
        f_ready_ve = s_ve.v
        emit_qpre()
        for c in (0, 1, 2, 3, 6, 7):
            emit_quant_c(c)

        e_evac = {}
        e_mm_stop = {}
        exp_done = {}
        max_done = {}
        a_done = {}
        out1_done = {}
        ln2_done = {}
        ln2T_evd = {}
        ffn_t_done = {}
        ffn_done = {}
        pT_evac = {}
        A_done = {}

        def emit_e_mms(s, pre_sweep=None, h_waits=None):
            for h in range(2):
                for kb in range(NKB):
                    e_evac[(s, h, kb)] = s_ve.v + h * NKB + kb + 1
            for h in range(2):
                for kb in range(NKB):
                    if pre_sweep and (h * NKB + kb) in pre_sweep:
                        pre_sweep[h * NKB + kb]()
                    for hc in range(NH):
                        wts = []
                        if hc == 0:
                            if h_waits is not None:
                                wts.append((s_ve, h_waits[h]))
                            if h == 1:
                                wts.append((s_ve, e_evac[(s, 0, kb)]))
                            elif s > 0:
                                wts.append((s_ve, e_evac[(s - 1, 1, kb)]))

                        def t_e(eng, s=s, h=h, hc=hc, kb=kb):
                            return eng.matmul(
                                pb[kb][:, 0:KW],
                                fT[:, hc * S + s * P:hc * S + (s + 1) * P],
                                fT[:, hc * S + h * HALF + kb * KW:
                                   hc * S + h * HALF + (kb + 1) * KW],
                                start=(hc == 0), stop=(hc == NH - 1))
                        v = PR.add("tensor", t_e, waits=wts,
                                   inc=s_t if hc == NH - 1 else None)
                        if hc == NH - 1:
                            e_mm_stop[(s, h, kb)] = v

        def emit_e_evacs(s):
            c0 = max(s * P - (MAXD - 1), 0)
            c1 = min(s * P + P + MAXD - 1, S)
            for h in range(2):
                for kb in range(NKB):
                    bi = h * NKB + kb
                    lo = h * HALF + kb * KW
                    hi = lo + KW
                    bc0 = max(lo, c0)
                    bc1 = min(hi, c1)
                    wts = [(s_t, e_mm_stop[(s, h, kb)])]
                    if s > 0 and h == 0 and kb == 0:
                        wts.append((s_sx, exp_done[s - 1]))
                    if s == 0 and h == 0 and kb == 0:
                        wts.append((s_vq, wprep["quant_e"]))

                    def s_ec(eng, kb=kb, lo=lo, hi=hi):
                        return eng.copy(e_sb[:, lo:hi], pb[kb][:, 0:KW])
                    got = PR.add("scalar", s_ec, waits=wts, inc=s_ve)
                    assert got == e_evac[(s, h, kb)], "evac reservation"

                    def v_pm(eng, s=s, lo=lo, hi=hi, bc0=bc0, bc1=bc1,
                             bi=bi):
                        if bc1 > bc0:
                            u0 = bc0 - (s * P - (MAXD - 1))
                            eng.tensor_tensor(
                                e_sb[:, bc0:bc1], e_sb[:, bc0:bc1],
                                band[:, u0:u0 + (bc1 - bc0)], ALU.add)
                        return eng.reduce_max(st_(PM0 + bi), e_sb[:, lo:hi],
                                              AXX)
                    PR.add("vector", v_pm,
                           waits=[(s_ve, e_evac[(s, h, kb)])], inc=s_vmax)

            def v_max(eng, s=s):
                return eng.reduce_max(st_(NMAX0 + s % 2),
                                      stats[:, PM0:PM0 + 2 * NKB], AXX,
                                      negate=True)
            max_done[s] = PR.add("vector", v_max, inc=s_vmax)

        def emit_exp(s):
            sc_w = [(s_vmax, max_done[s])]
            if s == 0:
                sc_w.append((s_vq, wprep["quant_p"]))
            if s > 0:
                sc_w.append((s_t, A_done[s - 1]))

            def s_exp(eng, s=s):
                return eng.activation(p_sb[:, 0:S], e_sb[:, 0:S], AF.Exp,
                                      bias=st_(NMAX0 + s % 2), scale=1.0,
                                      accum_out=st_(LSUM0 + s % 2))
            exp_done[s] = PR.add("scalar", s_exp, waits=sc_w, inc=s_sx)

        e0_h0 = max(pair_last[0], pair_last[1], pair_last[2], pair_last[3],
                    f_h0_ve[0])
        emit_e_mms(0, h_waits=[e0_h0, f_ready_ve])
        emit_e_evacs(0)
        emit_exp(0)
        for c in (4, 5):
            emit_quant_c(c)

        all_1c = c1_half0 + list(range(NSH, NST))
        onec_state = [True, 0]

        def pop_1c(n, eng=None):
            for _ in range(n):
                if not all_1c:
                    return
                stt = all_1c.pop(0)
                fw = None
                if stt >= NSH and onec_state[0]:
                    fw = (s_ve, f_ready_ve)
                    onec_state[0] = False
                i = onec_state[1]
                onec_state[1] += 1
                emit_1c(stt, fw, evac_eng=eng or
                        ("scalar" if i % 2 == 0 else "vector"))

        f_done = [(s_vp, s_vp.v), (s_sp2, s_sp2.v)]

        NJT = NST
        npack = (NJT + 7) // 8
        ss_ln2 = {}
        d_packs = {}
        lnp = {}

        for s in range(NST + 1):
            live = s < NST
            def emit_A_pack(s, g):
                j0 = 8 * g
                n = min(8, NJT - j0)
                srcs = [p_sb[:, (j0 + j) * P:(j0 + j + 1) * P]
                        for j in range(n)]
                dst = pT_sb[:, j0 * P:(j0 + n) * P]
                transpose_pack(srcs, dst,
                               extra_waits=[(s_sx, exp_done[s])])
                pT_evac[(s, g)] = s_vp.v

            if s >= 1 and s < NST:
                ss_ln2[s - 1] = ln_stat((s - 1) % 2, out1[(s - 1) % 2][:, :],
                                        sqscr[:, 0:D], [])
            elif s == NST:
                c0 = 8 * ((s - 1) % 2)

                def v_comb(eng, c0=c0):
                    eng.tensor_tensor(st_(c0 + SSUM), st_(ABS8),
                                      st_(ABS8 + 1), ALU.add)
                    eng.tensor_scalar_mul(st_(c0 + NMEAN),
                                          st_(c0 + SSUM), -1.0 / D)
                    eng.tensor_tensor(st_(c0 + VARSUM), st_(ABS8 + 2),
                                      st_(ABS8 + 3), ALU.add)
                    eng.tensor_tensor(st_(c0 + VP), st_(c0 + NMEAN),
                                      st_(c0 + NMEAN), ALU.mult)
                    return eng.tensor_scalar(st_(c0 + T1), st_(c0 + VP),
                                             -1.0, EPS, ALU.mult, ALU.add)
                vc = PR.add("vector", v_comb,
                            waits=[(s_ss, lnp["sq"])], inc=s_vm)

                def s_comb(eng, c0=c0):
                    eng.activation(st_(c0 + STD), st_(c0 + VARSUM),
                                   AF.Ln, bias=st_(c0 + T1), scale=1.0 / D)
                    return eng.activation(st_(c0 + R0), st_(c0 + STD),
                                          AF.Exp, bias=0.0, scale=-0.5)
                ss_ln2[s - 1] = PR.add("scalar", s_comb,
                                       waits=[(s_vm, vc)], inc=s_ss)

            if live and s + 1 < NST:
                if s > 0:
                    pre = {k + 1: (lambda s=s, g=k: emit_A_pack(s, g))
                           for k in range(npack)}
                else:
                    pre = None
                emit_e_mms(s + 1, pre_sweep=pre)
                if s == 0:
                    pop_1c(99, eng="vector")
                    f_done[:] = [(s_vp, s_vp.v), (s_sp2, s_sp2.v)]
                    for g in range(npack):
                        emit_A_pack(s, g)
                A_done[s] = s_t.v
            elif live:
                emit_warm(12, bank=0,
                          waits=[(s_ve, e_evac[(NST - 1, 1, NKB - 1)])])
                for g in range(npack):
                    emit_A_pack(s, g)
                A_done[s] = s_t.v

            if s >= 1:
                if s >= NST - 1:
                    ww = [(s_ve, e_evac[(NST - 1, 1, NKB - 1)])]
                    if s == NST:
                        emit_warm(10, bank=1,
                                  waits=ww + [(s_vo, out1_done[NST - 1])])
                        ww = ww + [(s_ss, ss_ln2[s - 1])]
                    emit_warm(14 if s == NST - 1 else 20, bank=0, waits=ww)
                aw = [(s_t, d_packs[s - 2])] if s >= 2 else []
                ln2_done[s - 1] = ln_apply((s - 1) % 2,
                                           out1[(s - 1) % 2][:, :],
                                           ln2[:, :], ss_ln2[s - 1], aw)
                srcs = [ln2[:, j * P:(j + 1) * P] for j in range(NH)]
                transpose_pack(srcs, ln2T[:, 0:D],
                               extra_waits=[(s_vln, ln2_done[s - 1])])
                d_packs[s - 1] = s_t.v
                ln2T_evd[s - 1] = s_vp.v

            if live and s + 1 < NST:
                emit_e_evacs(s + 1)
                emit_exp(s + 1)

            if live:
                a_done_nb = {}
                for nb in range(NBD):
                    for jt in range(NJT):
                        wts = []
                        if nb == 0 and jt == 0:
                            wts.append((s_vp, pT_evac[(s, 0)]))
                            wts.extend(f_done)
                            wts.append((s_vq, wprep['quant']))
                            if s == 0:
                                wts.append((s_ve, f_ready_ve))
                            if s > 0:
                                wts.append((s_vo, out1_done[s - 1]))
                            if s > 1:
                                wts.append((s_vf, ffn_done[s - 2]))
                        elif nb == 0 and jt % 8 == 0:
                            wts.append((s_vp, pT_evac[(s, jt // 8)]))

                        def t_a(eng, s=s, jt=jt, nb=nb):
                            return eng.matmul(
                                fw_ps[nb],
                                pT_sb[:, jt * P:(jt + 1) * P],
                                f_sb[:, jt * D + nb * 512:
                                     jt * D + (nb + 1) * 512],
                                start=(jt == 0), stop=(jt == NJT - 1))
                        PR.add("tensor", t_a, waits=wts,
                               inc=s_t if jt == NJT - 1 else None)
                    a_done_nb[nb] = s_t.v

                if s == 0:
                    xw = [(s_vln, ln_done[NST - 2])]
                elif s == 1:
                    xw = [(s_vln, ln_done[NST - 1])]
                else:
                    xw = [(s_vo, out1_done[s - 2])]
                xl = x_load(NST + s, xw)

                out1_nb = {}
                for nb in range(NBD):
                    def v_o1(eng, s=s, nb=nb):
                        if nb == 0:
                            eng.reciprocal(st_(RL), st_(LSUM0 + s % 2))
                        return eng.scalar_tensor_tensor(
                            out1[s % 2][:, nb * 512:(nb + 1) * 512],
                            fw_ps[nb], st_(RL),
                            xb[s % 2][:, nb * 512:(nb + 1) * 512],
                            ALU.mult, ALU.add)
                    w = [(s_t, a_done_nb[nb])]
                    if nb == 0:
                        w += [xl, (s_sx, exp_done[s])]
                    out1_nb[nb] = PR.add("vector", v_o1, waits=w, inc=s_vo)
                    if s == NST - 1:
                        def v_pp(eng, s=s, nb=nb):
                            return eng.reduce_sum(
                                st_(ABS8 + nb),
                                out1[s % 2][:, nb * 512:(nb + 1) * 512],
                                AXX)
                        PR.add("vector", v_pp, inc=s_vm)

                        def s_pp(eng, s=s, nb=nb):
                            return eng.activation(
                                sqscr[:, 0:512],
                                out1[s % 2][:, nb * 512:(nb + 1) * 512],
                                AF.Square, bias=0.0, scale=1.0,
                                accum_out=st_(ABS8 + 2 + nb))
                        lnp["sq"] = PR.add("scalar", s_pp,
                                           waits=[(s_vo, out1_nb[nb])],
                                           inc=s_ss)
                out1_done[s] = out1_nb[NBD - 1]

            if s >= 1:
                sm1 = s - 1

                wts_nb = {}
                for nb in range(NBD):
                    if live:
                        ow = out1_nb[nb]
                    else:
                        ow = out1_done[NST - 1]
                    wts = [(s_vo, ow)]
                    if nb == 0:
                        wts += [(s_vp, ln2T_evd[sm1]),
                                (s_vq, wprep['quant_last'])]
                    wts_nb[nb] = wts

                ffn_t_nb = {}
                for nb in range(NBD):
                    def t_ffn(eng, sm1=sm1, nb=nb):
                        inst = None
                        for ic in range(NH):
                            inst = eng.matmul(
                                fw_ps[nb],
                                ln2T[:, ic * P:(ic + 1) * P],
                                wqT[:, ic * D + nb * 512:
                                    ic * D + (nb + 1) * 512],
                                start=(ic == 0), stop=(ic == NH - 1))
                        return inst
                    ffn_t_nb[nb] = PR.add("tensor", t_ffn,
                                          waits=wts_nb[nb], inc=s_t)
                ffn_t_done[sm1] = ffn_t_nb[NBD - 1]

                if sm1 == NST - 1:
                    for nb in range(NBD):
                        fwc = [(s_t, ffn_t_nb[nb])]
                        if nb == 0:
                            fwc.append((dsem_o, dsem_o.v))

                        def v_fc(eng, sm1=sm1, nb=nb):
                            return eng.scalar_tensor_tensor(
                                outb[:, nb * 512:(nb + 1) * 512],
                                fw_ps[nb], st_(SCALE),
                                out1[sm1 % 2][:, nb * 512:(nb + 1) * 512],
                                ALU.mult, ALU.add)
                        fd = PR.add("vector", v_fc, waits=fwc, inc=s_vf)

                        def y_c(eng, sm1=sm1, nb=nb):
                            return eng.dma_start(
                                out_d[sm1 * P:(sm1 + 1) * P,
                                      nb * 512:(nb + 1) * 512],
                                outb[:, nb * 512:(nb + 1) * 512])
                        PR.add("sync", y_c, waits=[(s_vf, fd)],
                               inc=dsem_o, inc_amt=16)
                    ffn_done[sm1] = fd
                else:
                    fw = [(s_t, ffn_t_done[sm1]), (dsem_o, dsem_o.v)]

                    def v_final(eng, sm1=sm1):
                        inst = None
                        for nb in range(NBD):
                            inst = eng.scalar_tensor_tensor(
                                outb[:, nb * 512:(nb + 1) * 512],
                                fw_ps[nb], st_(SCALE),
                                out1[sm1 % 2][:, nb * 512:(nb + 1) * 512],
                                ALU.mult, ALU.add)
                        return inst
                    ffn_done[sm1] = PR.add("vector", v_final, waits=fw,
                                           inc=s_vf)

                    def y_out(eng, sm1=sm1):
                        return eng.dma_start(
                            out_d[sm1 * P:(sm1 + 1) * P, :], outb[:, :])
                    PR.add("sync", y_out, waits=[(s_vf, ffn_done[sm1])],
                           inc=dsem_o, inc_amt=16)

        PR.raw_wait("sync", dsem_o, dsem_o.v)

        PR.check()

        with nc.Block() as block:
            @block.tensor
            def _(eng):
                PR.emit(eng, "tensor", gsems.get("tensor"))

            @block.vector
            def _(eng):
                PR.emit(eng, "vector", gsems.get("vector"))

            @block.scalar
            def _(eng):
                PR.emit(eng, "scalar", gsems.get("scalar"))

            @block.sync
            def _(eng):
                PR.emit(eng, "sync", gsems.get("sync"))

            @block.gpsimd
            def _(eng):
                PR.emit(eng, "gpsimd", gsems.get("gpsimd"))

    return nc


_NC_CACHE = {}


def get_nc(S=4096):
    if S not in _NC_CACHE:
        _NC_CACHE[S] = build_nc(S)
    return _NC_CACHE[S]


def kernel(L0_attn_x, L0_attn_W, L0_attn_b, L0_ffn_weight):
    x = np.ascontiguousarray(np.asarray(L0_attn_x))
    W = np.asarray(L0_attn_W)
    b = np.ascontiguousarray(np.asarray(L0_attn_b))
    wf = np.asarray(L0_ffn_weight)
    wt = np.ascontiguousarray(W.T)
    ft = np.ascontiguousarray(wf.T)

    B, S, Dm = x.shape
    assert Dm == D and B == 8
    nc = get_nc(S)
    in_maps = [{"x": x[c], "wt": wt, "b": b, "ft": ft} for c in range(B)]
    res = run_bass_kernel_spmd(nc, in_maps, core_ids=list(range(B)))
    return np.stack([np.asarray(r["out"]) for r in res.results], axis=0)


# version 45
# speedup vs baseline: 1.0032x; 1.0023x over previous
import contextlib
import numpy as np
import concourse.bass as bass
import concourse.mybir as mybir
from concourse.bass_utils import run_bass_kernel_spmd

F32 = mybir.dt.float32
BF16 = mybir.dt.bfloat16
AF = mybir.ActivationFunctionType
ALU = mybir.AluOpType
AXX = mybir.AxisListType.X

P = 128
D = 1024
NH = D // P
NBD = D // 512
EPS = 1e-5
MAXD = 10
BSCALE = 0.01
BW = P + 2 * MAXD - 2
KiB = 1024


class SemCtr:
    def __init__(self, h, name):
        self.h = h
        self.name = name
        self.v = 0

    def bump(self, amt=1):
        self.v += amt
        return self.v


class Prog:
    ENGINES = ("tensor", "vector", "scalar", "sync", "gpsimd")

    def __init__(self):
        self.streams = {e: [] for e in self.ENGINES}
        self.all_waits = []

    def add(self, eng, fn, waits=(), inc=None, inc_amt=1):
        waits = [(s, v) for (s, v) in waits if v > 0]
        self.all_waits.extend(waits)
        post = inc.bump(inc_amt) if inc is not None else None
        self.streams[eng].append((fn, waits, inc, inc_amt, post))
        return post

    def raw_wait(self, eng, sem, val):
        self.streams[eng].append((lambda e: None, [(sem, val)], None, 1, None))

    def emit(self, eng_obj, name, guard_sem=None):
        guard = None
        if guard_sem is not None:
            guard = _ChainGuard(eng_obj, guard_sem)
            eng_obj = guard
        waited = {}
        for fn, waits, inc, inc_amt, post in self.streams[name]:
            for (s, v) in waits:
                if waited.get(s.name, -1) >= v:
                    continue
                waited[s.name] = v
                eng_obj.wait_ge(s.h, v)
            inst = fn(eng_obj)
            if inc is not None:
                assert inst is not None
                if guard is not None and inc_amt != 16:
                    guard.commit_group(inst, inc, post)
                else:
                    inst.then_inc(inc.h, inc_amt)

    def check(self):
        for (s, v) in self.all_waits:
            assert v <= s.v, f"wait {s.name}>={v} exceeds final {s.v}"


class _ChainGuard:

    def __init__(self, eng, sem):
        self._eng = eng
        self._sem = sem
        self._n = 0
        self._prev = None
        self._marker = None

    def _flush_prev(self):
        if self._prev is not None:
            self._prev.then_inc(self._sem.h, 1)
            self._n += 1
            self._marker = (self._sem.h, self._n)
            self._prev = None

    def commit_group(self, inst, inc, post_val):
        assert inst is self._prev or self._prev is None
        self._prev = None
        inst.then_inc(inc.h, 1)
        self._marker = (inc.h, post_val)

    def wait_ge(self, *a, **k):
        return self._eng.wait_ge(*a, **k)

    def dma_start(self, *a, **k):
        return self._eng.dma_start(*a, **k)

    def __getattr__(self, name):
        real = getattr(self._eng, name)
        if not callable(real):
            return real

        def wrap(*a, **k):
            self._flush_prev()
            if self._marker is not None:
                self._eng.wait_ge(self._marker[0], self._marker[1])
            inst = real(*a, **k)
            self._prev = inst
            return inst
        return wrap


def build_nc(S, sim_mode=False):
    NST = S // P
    HALF = S // 2
    NSH = NST // 2
    KW = min(512, HALF)
    NKB = HALF // KW
    assert NKB <= 4
    CHW = min(512, HALF)
    NHH = HALF // CHW
    SPW = NSH // NHH

    nc = bass.Bass("TRN2", target_bir_lowering=False,
                   detect_race_conditions=sim_mode)

    x_d = nc.dram_tensor("x", [S, D], F32, kind="ExternalInput").ap()
    wt_d = nc.dram_tensor("wt", [D, D], F32, kind="ExternalInput").ap()
    b_d = nc.dram_tensor("b", [D], F32, kind="ExternalInput").ap()
    ft_d = nc.dram_tensor("ft", [D, D], F32, kind="ExternalInput").ap()
    out_d = nc.dram_tensor("out", [S, D], F32, kind="ExternalOutput").ap()

    off = {}
    cur = 0

    def place(name, nbytes, align=512):
        nonlocal cur
        cur = (cur + align - 1) // align * align
        off[name] = cur
        cur += nbytes

    place("fT", NH * S * 2)
    f_base = cur
    place("f", NST * D * 2)
    e_base = cur
    place("e", S * 4)
    pT_base = cur
    place("pT", S * 2)
    place("p", S * 2)
    if S >= 4096:
        off["wstage"] = e_base
        off["WT"] = f_base + 32 * KiB
        off["ln1T0"] = f_base
        off["wtstage"] = f_base + 48 * KiB
        off["ln1Tr"] = f_base + 48 * KiB
        assert NH * HALF * 2 <= 32 * KiB and 64 * KiB <= NST * D * 2
        assert 2 * NH * CHW * 2 <= 16 * KiB
    else:
        place("wstage", NH * D * 4)
        place("WT", NH * D * 2)
        place("wtstage", 4 * D * 4)
        place("ln1T0", NH * HALF * 2)
        place("ln1Tr", 2 * NH * CHW * 2)
    place("xbuf0", D * 4)
    place("xbuf1", D * 4)
    lnbuf_base = cur
    place("lnbuf0", D * 2)
    place("lnbuf1", D * 2)
    place("out1a", D * 4)
    place("outbuf", D * 4)
    place("ln2", D * 2)
    place("ln2T", D * 2)
    place("wqT", NH * D * 2)
    off["wtstage2"] = off["wqT"]
    assert 4 * D * 4 <= NH * D * 2
    place("b_sb", NH * 4)
    place("consts", 8 * 4)
    place("stats", 48 * 4)
    place("id", P * 2)
    place("band", BW * 4)
    place("iotascratch", BW * 4)
    place("ones", P * 4)
    place("sqscr", D * 2)
    off["out1b"] = lnbuf_base
    ARENA_BASE = 16512
    assert ARENA_BASE + cur <= 229344, f"arena overflow: {cur}"

    with contextlib.ExitStack() as ctx:
        pb = [ctx.enter_context(nc.psum_tensor(f"pb{i}", [P, 512], F32))
              for i in range(6)]
        pbt = [ctx.enter_context(nc.psum_tensor(f"pbt{i}", [P, 1024], BF16))
               for i in range(2)]

        def view(name, cols, dt):
            h = nc.alloc_sbuf_tensor_at(name, [P, cols], dt,
                                        offset=ARENA_BASE + off[name])
            return h[:, :]

        fT = view("fT", NH * S, BF16)
        f_sb = view("f", NST * D, BF16)
        wstage = view("wstage", NH * D, F32)
        wtstage = view("wtstage", 4 * D, F32)
        wtstage2 = view("wtstage2", 4 * D, F32)
        WT = view("WT", NH * D, BF16)
        ln1T0 = view("ln1T0", NH * HALF, BF16)
        ln1Tr = view("ln1Tr", 2 * NH * CHW, BF16)
        e_sb = view("e", S, F32)
        pT_sb = view("pT", S, BF16)
        p_sb = view("p", S, BF16)
        xb = [view("xbuf0", D, F32), view("xbuf1", D, F32)]
        lnb = [view("lnbuf0", D, BF16), view("lnbuf1", D, BF16)]
        out1 = [view("out1a", D, F32), view("out1b", D, F32)]
        outb = view("outbuf", D, F32)
        xb4 = xb + [out1[0], outb]
        sqscr = view("sqscr", D, BF16)
        ln2 = view("ln2", D, BF16)
        ln2T = view("ln2T", D, BF16)
        wqT = view("wqT", NH * D, BF16)
        b_sb = view("b_sb", NH, F32)
        consts = view("consts", 8, F32)
        stats = view("stats", 48, F32)
        id_sb = view("id", P, BF16)
        band = view("band", BW, F32)
        iosc = view("iotascratch", BW, F32)
        ones_sb = view("ones", P, F32)

        SSUM, NMEAN, VARSUM, STD, VP, R0, T1, RSTD = range(8)
        LSUM0, LSUM1, RL, SCALE, HTH, NHTH, ABSP, TOT = range(16, 24)
        PM0 = 24
        NMAX0, NMAX1, ROWI, ROWB = 32, 33, 34, 35
        ABS8 = 40

        def st_(c):
            return stats[:, c:c + 1]

        twb = [pbt[0][:, 0:1024], pbt[1][:, 0:1024]]
        fw_ps = [pb[4][:, 0:512], pb[5][:, 0:512]]

        def mksem(name):
            return SemCtr(ctx.enter_context(nc.semaphore(name)), name)

        dsem_x = [mksem("dx0"), mksem("dx1"), mksem("dx2"), mksem("dx3")]
        dsem_xg = [mksem("dxg1"), mksem("dxg3")]
        dsem_ft = [mksem(f"dft{i}") for i in range(4)]
        dsem_b = mksem("dbb")
        dsem_w = [mksem(f"dw{i}") for i in range(8)]
        dsem_o = mksem("do")
        s_t = mksem("st")
        s_vp = mksem("svp")
        s_ve = mksem("sve")
        s_vo = mksem("svo")
        s_vm = mksem("svm")
        s_vln = mksem("svln")
        s_vmax = mksem("svmax")
        s_vf = mksem("svf")
        s_ss = mksem("sss")
        s_sx = mksem("ssx")
        s_gp = mksem("sgp")
        s_vq = mksem("svq")
        s_sp2 = mksem("ssp2")

        gsems = {}
        for e in ("vector", "scalar", "gpsimd", "sync"):
            gsems[e] = mksem("guard_" + e)
        if sim_mode:
            gsems["tensor"] = mksem("guard_tensor")

        PR = Prog()

        CONST_VALS = [0.0, 1.0, EPS, 1.0 / D, -0.5]
        for ci, cv in enumerate(CONST_VALS):
            nc.const_aps.register(F32, cv, consts[:, ci:ci + 1])

        def v_consts(eng):
            for ci, cv in enumerate(CONST_VALS):
                eng.memset(consts[:, ci:ci + 1], cv)
            return eng.memset(ones_sb[:, 0:P], 1.0)
        PR.add("vector", v_consts, inc=s_vq)

        def g_iota(eng):
            eng.iota(iosc[:, 0:P], [[1, P]], channel_multiplier=0,
                     allow_small_or_imprecise_dtypes=True)
            eng.iota(st_(ROWI), [[1, 1]], channel_multiplier=1,
                     allow_small_or_imprecise_dtypes=True)
            eng.iota(band[:, 0:BW], [[1, BW]], channel_multiplier=0,
                     allow_small_or_imprecise_dtypes=True)
            return eng.iota(st_(ROWB), [[1, 1]], base=MAXD - 1,
                            channel_multiplier=1,
                            allow_small_or_imprecise_dtypes=True)
        g_iota_done = PR.add("gpsimd", g_iota, inc=s_gp)

        def g_b(eng):
            return eng.dma_start(b_sb[:, 0:NH],
                                 b_d.rearrange("(c p) -> p c", p=P),
                                 allow_slow_non_contiguous=True)
        b_loaded = PR.add("gpsimd", g_b, inc=dsem_b, inc_amt=16)

        ftg_waits = {}

        def emit_ft_issue(c):
            def g_ft(eng, c=c):
                return eng.dma_start(wstage[:, c * D:(c + 1) * D],
                                     ft_d[c * P:(c + 1) * P, :])
            sem = dsem_ft[c // 2]
            PR.add("gpsimd" if c < 4 else "scalar", g_ft,
                   inc=sem, inc_amt=16)
            if c % 2 == 1:
                ftg_waits[c // 2] = (sem, sem.v)

        def v_id(eng):
            eng.tensor_scalar(id_sb[:, 0:P], iosc[:, 0:P], st_(ROWI), None,
                              ALU.is_equal)
            eng.tensor_scalar(band[:, 0:BW], band[:, 0:BW], st_(ROWB), None,
                              ALU.subtract)
            eng.tensor_scalar_add(iosc[:, 0:BW], band[:, 0:BW], float(MAXD))
            eng.tensor_scalar(band[:, 0:BW], band[:, 0:BW], -1.0,
                              float(MAXD), ALU.mult, ALU.add)
            eng.tensor_tensor(band[:, 0:BW], band[:, 0:BW], iosc[:, 0:BW],
                              ALU.min)
            eng.tensor_scalar_max(band[:, 0:BW], band[:, 0:BW], 0.0)
            return eng.tensor_scalar_mul(band[:, 0:BW], band[:, 0:BW],
                                         -BSCALE)
        id_done = PR.add("vector", v_id, waits=[(s_gp, g_iota_done)], inc=s_vq)

        def emit_warm(n, bank=5, waits=()):
            def t_warm(eng, n=n, bank=bank):
                inst = None
                for _ in range(n):
                    inst = eng.matmul(pb[bank][:, 0:P], ones_sb[:, 0:P],
                                      ones_sb[:, 0:P], start=True, stop=True)
                return inst
            PR.add("tensor", t_warm, waits=list(waits), inc=s_t)

        wprep = {}
        cast_ms = {}
        wt_lands = {}

        def wt_slot_ap(c):
            stg = wtstage if c < 4 else wtstage2
            sl = c % 4
            return stg[:, sl * D:(sl + 1) * D]

        def emit_wt_issue(c):
            sem = dsem_w[c]

            def fn(eng, c=c):
                return eng.dma_start(wt_slot_ap(c),
                                     wt_d[c * P:(c + 1) * P, :])
            PR.add("scalar" if c % 2 == 0 else "gpsimd", fn,
                   inc=sem, inc_amt=16)
            wt_lands[c] = (sem, sem.v)

        def emit_cast_c(c):
            def fn(eng, c=c):
                return eng.activation(WT[:, c * D:(c + 1) * D],
                                      wt_slot_ap(c),
                                      AF.Identity, bias=0.0, scale=1.0)
            cast_ms[c] = PR.add("scalar", fn, waits=[wt_lands[c]], inc=s_ss)
            if c == NH - 1:
                wprep["cast_last"] = cast_ms[c]

        absum_ms = {}

        def emit_absum_g(g):
            def v_absum(eng, g=g):
                inst = None
                for c in (2 * g, 2 * g + 1):
                    inst = eng.reduce_sum(st_(ABS8 + c),
                                          wstage[:, c * D:(c + 1) * D], AXX,
                                          apply_absolute_value=True)
                return inst
            absum_ms[g] = PR.add("vector", v_absum, waits=[ftg_waits[g]],
                                 inc=s_vq)

        def emit_allred_g(g, first=False, last=False):
            def t_allred(eng, g=g):
                return eng.matmul(pb[4][:, 0:2], ones_sb[:, 0:P],
                                  stats[:, ABS8 + 2 * g:ABS8 + 2 * g + 2],
                                  start=first, stop=last)
            done = PR.add("tensor", t_allred,
                          waits=[(s_vq, absum_ms[g])], inc=s_t)
            if last:
                wprep["allred"] = done

        def emit_qpre():
            def v_qpre(eng):
                eng.tensor_copy(stats[:, ABS8:ABS8 + 2], pb[4][:, 0:2])
                eng.reduce_sum(st_(TOT), stats[:, ABS8:ABS8 + 2], AXX)
                eng.tensor_scalar(st_(SCALE), st_(TOT), 1.0 / (D * D), EPS,
                                  ALU.mult, ALU.max)
                eng.tensor_scalar_mul(st_(HTH), st_(SCALE), 0.5)
                return eng.tensor_scalar_mul(st_(NHTH), st_(SCALE), -0.5)
            wprep["quant"] = PR.add("vector", v_qpre,
                                    waits=[(s_t, wprep["allred"])], inc=s_vq)

        def emit_quant_c(c):
            def v_q(eng, c=c):
                eng.tensor_single_scalar(wqT[:, c * D:(c + 1) * D],
                                         wstage[:, c * D:(c + 1) * D],
                                         st_(NHTH), ALU.is_lt)
                return eng.scalar_tensor_tensor(
                    wqT[:, c * D:(c + 1) * D],
                    wstage[:, c * D:(c + 1) * D], st_(HTH),
                    wqT[:, c * D:(c + 1) * D], ALU.is_gt, ALU.subtract)
            waits = [(s_ss, wprep["cast_last"])] \
                if not wprep.get("quant_first") else []
            done = PR.add("vector", v_q, waits=waits, inc=s_vq)
            wprep["quant_first"] = True
            if c == 3:
                wprep["quant_e"] = done
            if c == 7:
                wprep["quant_p"] = done
            if c == 5:
                wprep["quant_last"] = done

        def x_load(k, waits, nbuf=2):
            def fn(eng, k=k, nbuf=nbuf):
                st = k % NST
                buf = xb4[k % nbuf] if nbuf == 4 else xb[k % 2]
                return eng.dma_start(buf[:, :], x_d[st * P:(st + 1) * P, :])
            if nbuf == 4 and k % 2 == 1 and k >= 9:
                eng_name = "gpsimd"
                sem = dsem_xg[(k % 4) // 2]
            else:
                eng_name = "sync"
                sem = dsem_x[k % nbuf]
            PR.add(eng_name, fn, waits=waits, inc=sem, inc_amt=16)
            return (sem, sem.v)

        def ln_stat(par, xsrc, sq_scr, stat_waits):
            c0 = 8 * par

            def v_stat(eng):
                eng.reduce_sum(st_(c0 + SSUM), xsrc, AXX)
                return eng.tensor_scalar_mul(st_(c0 + NMEAN), st_(c0 + SSUM),
                                             -1.0 / D)
            vm = PR.add("vector", v_stat, waits=stat_waits, inc=s_vm)

            def s_var(eng):
                eng.activation(sq_scr, xsrc, AF.Square,
                               bias=st_(c0 + NMEAN), scale=1.0,
                               accum_out=st_(c0 + VARSUM))
                eng.activation(st_(c0 + STD), st_(c0 + VARSUM),
                               AF.Ln, bias=EPS, scale=1.0 / D)
                return eng.activation(st_(c0 + R0), st_(c0 + STD),
                                      AF.Exp, bias=0.0, scale=-0.5)
            return PR.add("scalar", s_var, waits=[(s_vm, vm)], inc=s_ss)

        def ln_apply(par, xsrc, dst_bf16, ss, apply_waits):
            c0 = 8 * par

            def v_apply(eng):
                return eng.tensor_scalar(dst_bf16, xsrc, st_(c0 + NMEAN),
                                         st_(c0 + R0), ALU.add, ALU.mult)
            return PR.add("vector", v_apply,
                          waits=[(s_ss, ss)] + list(apply_waits), inc=s_vln)

        def layernorm(par, xsrc, dst_bf16, sq_scr, stat_waits, apply_waits):
            ss = ln_stat(par, xsrc, sq_scr, stat_waits)
            return ln_apply(par, xsrc, dst_bf16, ss, apply_waits)

        twin_free = [(s_vp, 0), (s_vp, 0)]
        twin_i = [0]

        def transpose_pack(srcs, dst, dst_is_3d=False, extra_waits=(),
                           evac_eng="vector"):
            w = twin_i[0] % 2
            twin_i[0] += 1
            win = twb[w]
            waits = [twin_free[w]] + list(extra_waits)

            def t_fn(eng, srcs=srcs, win=win):
                inst = None
                for t, s in enumerate(srcs):
                    inst = eng.transpose(win[:, t * P:(t + 1) * P], s,
                                         id_sb[:, 0:P])
                return inst
            tv = PR.add("tensor", t_fn, waits=waits, inc=s_t)

            def v_fn(eng, dst=dst, win=win, n=len(srcs)):
                src = win[:, 0:n * P]
                if dst_is_3d:
                    src = src.rearrange("p (c x) -> p c x", c=n)
                if evac_eng == "vector":
                    return eng.tensor_copy(dst, src)
                return eng.activation(dst, src, AF.Identity,
                                      bias=0.0, scale=1.0)
            sem = s_vp if evac_eng == "vector" else s_sp2
            ev = PR.add(evac_eng, v_fn, waits=[(s_t, tv)], inc=sem)
            twin_free[w] = (sem, ev)
            return tv, ev

        ln_done = {}
        tpose_done = {}
        sp_evac = {}
        ss_of = {}
        fmm_evac = {}
        NGRP = NH * NHH

        def strip_stat(st):
            w = [(s_vln, ln_done[st - 4])] if st >= 4 else []
            xl = x_load(st, w, nbuf=4)
            ss_of[st] = ln_stat(st % 2, xb4[st % 4][:, :], sqscr[:, 0:D],
                                [xl])

        h1_win_evac = {}

        def strip_apply(st):
            if 0 < st <= 4:
                emit_warm(14)
            half = st // NSH
            sl = st % NSH
            hh = sl // SPW
            pos = sl % SPW
            aw = [(s_t, tpose_done[st - 2])] if st >= 2 else []
            ln_done[st] = ln_apply(st % 2, xb4[st % 4][:, :],
                                   lnb[st % 2][:, :], ss_of[st], aw)
            srcs = [lnb[st % 2][:, j * P:(j + 1) * P] for j in range(NH)]
            if half == 0:
                dst = ln1T0.rearrange("p (c x) -> p c x", c=NH)[
                    :, :, sl * P:(sl + 1) * P]
            else:
                slot = hh % 2
                dst = ln1Tr[:, slot * NH * CHW:(slot + 1) * NH * CHW
                            ].rearrange("p (c x) -> p c x", c=NH)[
                    :, :, pos * P:(pos + 1) * P]
            ew = [(s_vln, ln_done[st])]
            if st == 0:
                ew.append((s_vq, id_done))
            if half == 1:
                if hh == 0 and pos == 0 and S >= 4096:
                    ew.append((s_ss, wprep["cast_last"]))
                if hh >= 2 and pos == 0:
                    ew.append((s_ve, h1_win_evac[hh - 2]))
            transpose_pack(srcs, dst, dst_is_3d=True, extra_waits=ew)
            tpose_done[st] = s_t.v
            sp_evac[st] = s_vp.v

        fmm_seq = [0]
        f_h0_ve = [0]
        pair_last = {0: 0, 1: 0, 2: 0, 3: 0, 4: 0}

        def emit_fmm_group(half, gi, lnT_ready, pair_override=None):
            hc = gi // NHH
            hh = gi % NHH
            gkey = fmm_seq[0]
            fmm_seq[0] += 1
            if pair_override is not None:
                pair = pair_override
            else:
                pair = gkey % 4
            mmw = [(s_vp, lnT_ready), (s_ss, wprep['cast_last'])]
            if pair == 4 and pair_last[4] == 0:
                mmw.append((s_vq, wprep['quant']))
            if pair_last[pair] > 0:
                mmw.append((s_ve, pair_last[pair]))
            nnb = max(CHW // 512, 1)
            kw = min(512, CHW)

            def t_fmm(eng, hc=hc, hh=hh, pair=pair, half=half):
                inst = None
                for dc in range(NH):
                    for nb in range(nnb):
                        if half == 0:
                            base = dc * HALF + hh * CHW
                            lnt = ln1T0
                        else:
                            base = (hh % 2) * NH * CHW + dc * CHW
                            lnt = ln1Tr
                        inst = eng.matmul(
                            pb[pair + nb][:, 0:kw],
                            WT[:, dc * D + hc * P:dc * D + (hc + 1) * P],
                            lnt[:, base + nb * kw:base + (nb + 1) * kw],
                            start=(dc == 0), stop=(dc == NH - 1))
                return inst
            gdone = PR.add("tensor", t_fmm, waits=mmw, inc=s_t)

            def s_fevac(eng, hc=hc, hh=hh, pair=pair, half=half):
                inst = None
                for nb in range(nnb):
                    base = hc * S + half * HALF + hh * CHW + nb * kw
                    inst = eng.activation(
                        fT[:, base:base + kw], pb[pair + nb][:, 0:kw],
                        AF.Identity, bias=b_sb[:, hc:hc + 1], scale=1.0)
                return inst
            PR.add("scalar", s_fevac,
                   waits=[(s_t, gdone), (dsem_b, b_loaded)], inc=s_ve)
            fmm_evac[gkey] = s_ve.v
            pair_last[pair] = s_ve.v
            if half == 0:
                f_h0_ve[0] = s_ve.v
            else:
                h1_win_evac[hh] = s_ve.v

        SPW = NSH // NHH
        fmm_sched = {}
        fmm_tail = []
        for w in range(2 * NHH):
            h, hh = divmod(w, NHH)
            wlast = h * NSH + hh * SPW + (SPW - 1)
            for k, hc in enumerate(range(NH)):
                gi = hc * NHH + hh
                target = (w + 1) * SPW + k // 2
                if target < NST:
                    fmm_sched.setdefault(target, []).append((h, gi, wlast))
                else:
                    fmm_tail.append((h, gi, wlast))

        def run_sched(st):
            for (h, gi, wlast) in fmm_sched.get(st, []):
                emit_fmm_group(h, gi, sp_evac[wlast])

        prep_sched = {
            0: [lambda c=c: emit_cast_c(c) for c in range(4)],
            1: [lambda c=c: emit_cast_c(c) for c in range(4, NH)],
            2: [lambda c=c: emit_ft_issue(c) for c in range(6)],
            3: [lambda: emit_ft_issue(6), lambda: emit_ft_issue(7)],
        }
        def flush_prep(sa):
            for k in sorted(k for k in prep_sched if k <= sa):
                for th in prep_sched.pop(k):
                    th()

        for c in (0, 2, 4, 6):
            emit_wt_issue(c)
        for c in (1, 3, 5, 7):
            emit_wt_issue(c)
        strip_stat(0)
        strip_stat(1)
        strip_apply(0)
        flush_prep(0)
        run_sched(0)
        if NST > 2:
            strip_stat(2)
        strip_apply(1)
        run_sched(1)
        flush_prep(1)
        for st in range(3, NST + 1):
            if st < NST:
                strip_stat(st)
            strip_apply(st - 1)
            flush_prep(st - 1 if st < NST else 10 ** 9)
            run_sched(st - 1)

        def emit_1c(stt, first_wait=None, evac_eng=None):
            srcs = [fT[:, j * S + stt * P:j * S + (stt + 1) * P]
                    for j in range(NH)]
            dst = f_sb[:, stt * D:stt * D + D]
            ew = [first_wait] if first_wait else []
            if evac_eng is None:
                evac_eng = "scalar" if stt < NSH else "vector"
            transpose_pack(srcs, dst, extra_waits=ew, evac_eng=evac_eng)

        c1_half0 = list(range(NSH))
        qrem = [0, 1, 2, 3, 6, 7, 4, 5]
        first0 = True

        def next_1c():
            nonlocal first0
            if c1_half0:
                stt = c1_half0.pop(0)
                emit_1c(stt, ((s_ve, f_h0_ve[0]) if first0 else None))
                first0 = False

        for (h, gi, wlast) in [t for t in fmm_tail if t[0] == 0]:
            emit_fmm_group(h, gi, sp_evac[wlast])
        for g in range(4):
            emit_absum_g(g)
        nall = 0
        for i, (h, gi, wlast) in enumerate(
                [t for t in fmm_tail if t[0] == 1]):
            emit_fmm_group(h, gi, sp_evac[wlast])
            if 1 <= i <= 4:
                emit_allred_g(i - 1, first=(i == 1), last=(i == 4))
                nall = i
            next_1c()
        while nall < 4:
            emit_allred_g(nall, first=(nall == 0), last=(nall == 3))
            nall += 1
        f_ready_ve = s_ve.v
        emit_qpre()
        for c in (0, 1, 2, 3, 6, 7):
            emit_quant_c(c)

        e_evac = {}
        e_mm_stop = {}
        exp_done = {}
        max_done = {}
        a_done = {}
        out1_done = {}
        ln2_done = {}
        ln2T_evd = {}
        ffn_t_done = {}
        ffn_done = {}
        pT_evac = {}
        A_done = {}

        def emit_e_mms(s, pre_sweep=None, h_waits=None):
            for h in range(2):
                for kb in range(NKB):
                    e_evac[(s, h, kb)] = s_ve.v + h * NKB + kb + 1
            for h in range(2):
                for kb in range(NKB):
                    if pre_sweep and (h * NKB + kb) in pre_sweep:
                        pre_sweep[h * NKB + kb]()
                    for hc in range(NH):
                        wts = []
                        if hc == 0:
                            if h_waits is not None:
                                wts.append((s_ve, h_waits[h]))
                            if h == 1:
                                wts.append((s_ve, e_evac[(s, 0, kb)]))
                            elif s > 0:
                                wts.append((s_ve, e_evac[(s - 1, 1, kb)]))

                        def t_e(eng, s=s, h=h, hc=hc, kb=kb):
                            return eng.matmul(
                                pb[kb][:, 0:KW],
                                fT[:, hc * S + s * P:hc * S + (s + 1) * P],
                                fT[:, hc * S + h * HALF + kb * KW:
                                   hc * S + h * HALF + (kb + 1) * KW],
                                start=(hc == 0), stop=(hc == NH - 1))
                        v = PR.add("tensor", t_e, waits=wts,
                                   inc=s_t if hc == NH - 1 else None)
                        if hc == NH - 1:
                            e_mm_stop[(s, h, kb)] = v

        def emit_e_evacs(s):
            c0 = max(s * P - (MAXD - 1), 0)
            c1 = min(s * P + P + MAXD - 1, S)
            for h in range(2):
                for kb in range(NKB):
                    bi = h * NKB + kb
                    lo = h * HALF + kb * KW
                    hi = lo + KW
                    bc0 = max(lo, c0)
                    bc1 = min(hi, c1)
                    wts = [(s_t, e_mm_stop[(s, h, kb)])]
                    if s > 0 and h == 0 and kb == 0:
                        wts.append((s_sx, exp_done[s - 1]))
                    if s == 0 and h == 0 and kb == 0:
                        wts.append((s_vq, wprep["quant_e"]))

                    def s_ec(eng, kb=kb, lo=lo, hi=hi):
                        return eng.copy(e_sb[:, lo:hi], pb[kb][:, 0:KW])
                    got = PR.add("scalar", s_ec, waits=wts, inc=s_ve)
                    assert got == e_evac[(s, h, kb)], "evac reservation"

                    def v_pm(eng, s=s, lo=lo, hi=hi, bc0=bc0, bc1=bc1,
                             bi=bi):
                        if bc1 > bc0:
                            u0 = bc0 - (s * P - (MAXD - 1))
                            eng.tensor_tensor(
                                e_sb[:, bc0:bc1], e_sb[:, bc0:bc1],
                                band[:, u0:u0 + (bc1 - bc0)], ALU.add)
                        return eng.reduce_max(st_(PM0 + bi), e_sb[:, lo:hi],
                                              AXX)
                    PR.add("vector", v_pm,
                           waits=[(s_ve, e_evac[(s, h, kb)])], inc=s_vmax)

            def v_max(eng, s=s):
                return eng.reduce_max(st_(NMAX0 + s % 2),
                                      stats[:, PM0:PM0 + 2 * NKB], AXX,
                                      negate=True)
            max_done[s] = PR.add("vector", v_max, inc=s_vmax)

        def emit_exp(s):
            sc_w = [(s_vmax, max_done[s])]
            if s == 0:
                sc_w.append((s_vq, wprep["quant_p"]))
            if s > 0:
                sc_w.append((s_t, A_done[s - 1]))

            def s_exp(eng, s=s):
                return eng.activation(p_sb[:, 0:S], e_sb[:, 0:S], AF.Exp,
                                      bias=st_(NMAX0 + s % 2), scale=1.0,
                                      accum_out=st_(LSUM0 + s % 2))
            exp_done[s] = PR.add("scalar", s_exp, waits=sc_w, inc=s_sx)

        e0_h0 = max(pair_last[0], pair_last[1], pair_last[2], pair_last[3],
                    f_h0_ve[0])
        emit_e_mms(0, h_waits=[e0_h0, f_ready_ve])
        emit_e_evacs(0)
        emit_exp(0)
        for c in (4, 5):
            emit_quant_c(c)

        all_1c = c1_half0 + list(range(NSH, NST))
        onec_state = [True, 0]

        def pop_1c(n, eng=None):
            for _ in range(n):
                if not all_1c:
                    return
                stt = all_1c.pop(0)
                fw = None
                if stt >= NSH and onec_state[0]:
                    fw = (s_ve, f_ready_ve)
                    onec_state[0] = False
                i = onec_state[1]
                onec_state[1] += 1
                emit_1c(stt, fw, evac_eng=eng or
                        ("scalar" if i % 2 == 0 else "vector"))

        pop_1c(2)
        f_done = [(s_vp, s_vp.v), (s_sp2, s_sp2.v)]

        NJT = NST
        npack = (NJT + 7) // 8
        ss_ln2 = {}
        d_packs = {}
        lnp = {}

        for s in range(NST + 1):
            live = s < NST
            def emit_A_pack(s, g):
                j0 = 8 * g
                n = min(8, NJT - j0)
                srcs = [p_sb[:, (j0 + j) * P:(j0 + j + 1) * P]
                        for j in range(n)]
                dst = pT_sb[:, j0 * P:(j0 + n) * P]
                transpose_pack(srcs, dst,
                               extra_waits=[(s_sx, exp_done[s])])
                pT_evac[(s, g)] = s_vp.v

            if s >= 1 and s < NST:
                ss_ln2[s - 1] = ln_stat((s - 1) % 2, out1[(s - 1) % 2][:, :],
                                        sqscr[:, 0:D], [])
            elif s == NST:
                c0 = 8 * ((s - 1) % 2)

                def v_comb(eng, c0=c0):
                    eng.tensor_tensor(st_(c0 + SSUM), st_(ABS8),
                                      st_(ABS8 + 1), ALU.add)
                    eng.tensor_scalar_mul(st_(c0 + NMEAN),
                                          st_(c0 + SSUM), -1.0 / D)
                    eng.tensor_tensor(st_(c0 + VARSUM), st_(ABS8 + 2),
                                      st_(ABS8 + 3), ALU.add)
                    eng.tensor_tensor(st_(c0 + VP), st_(c0 + NMEAN),
                                      st_(c0 + NMEAN), ALU.mult)
                    return eng.tensor_scalar(st_(c0 + T1), st_(c0 + VP),
                                             -1.0, EPS, ALU.mult, ALU.add)
                vc = PR.add("vector", v_comb,
                            waits=[(s_ss, lnp["sq"])], inc=s_vm)

                def s_comb(eng, c0=c0):
                    eng.activation(st_(c0 + STD), st_(c0 + VARSUM),
                                   AF.Ln, bias=st_(c0 + T1), scale=1.0 / D)
                    return eng.activation(st_(c0 + R0), st_(c0 + STD),
                                          AF.Exp, bias=0.0, scale=-0.5)
                ss_ln2[s - 1] = PR.add("scalar", s_comb,
                                       waits=[(s_vm, vc)], inc=s_ss)

            if live and s + 1 < NST:
                if s > 0:
                    pre = {k + 1: (lambda s=s, g=k: emit_A_pack(s, g))
                           for k in range(npack)}
                else:
                    pre = None
                emit_e_mms(s + 1, pre_sweep=pre)
                if s == 0:
                    pop_1c(99, eng="vector")
                    f_done[:] = [(s_vp, s_vp.v), (s_sp2, s_sp2.v)]
                    for g in range(npack):
                        emit_A_pack(s, g)
                A_done[s] = s_t.v
            elif live:
                emit_warm(16, bank=0,
                          waits=[(s_ve, e_evac[(NST - 1, 1, NKB - 1)])])
                for g in range(npack):
                    emit_A_pack(s, g)
                A_done[s] = s_t.v

            if s >= 1:
                if s >= NST - 1:
                    ww = [(s_ve, e_evac[(NST - 1, 1, NKB - 1)])]
                    if s == NST:
                        emit_warm(16, bank=1,
                                  waits=ww + [(s_vo, out1_done[NST - 1])])
                        ww = ww + [(s_ss, ss_ln2[s - 1])]
                    emit_warm(14 if s == NST - 1 else 20, bank=0, waits=ww)
                aw = [(s_t, d_packs[s - 2])] if s >= 2 else []
                ln2_done[s - 1] = ln_apply((s - 1) % 2,
                                           out1[(s - 1) % 2][:, :],
                                           ln2[:, :], ss_ln2[s - 1], aw)
                srcs = [ln2[:, j * P:(j + 1) * P] for j in range(NH)]
                transpose_pack(srcs, ln2T[:, 0:D],
                               extra_waits=[(s_vln, ln2_done[s - 1])])
                d_packs[s - 1] = s_t.v
                ln2T_evd[s - 1] = s_vp.v

            if live and s + 1 < NST:
                emit_e_evacs(s + 1)
                emit_exp(s + 1)

            if live:
                a_done_nb = {}
                for nb in range(NBD):
                    for jt in range(NJT):
                        wts = []
                        if nb == 0 and jt == 0:
                            wts.append((s_vp, pT_evac[(s, 0)]))
                            wts.extend(f_done)
                            wts.append((s_vq, wprep['quant']))
                            if s == 0:
                                wts.append((s_ve, f_ready_ve))
                            if s > 0:
                                wts.append((s_vo, out1_done[s - 1]))
                            if s > 1:
                                wts.append((s_vf, ffn_done[s - 2]))
                        elif nb == 0 and jt % 8 == 0:
                            wts.append((s_vp, pT_evac[(s, jt // 8)]))

                        def t_a(eng, s=s, jt=jt, nb=nb):
                            return eng.matmul(
                                fw_ps[nb],
                                pT_sb[:, jt * P:(jt + 1) * P],
                                f_sb[:, jt * D + nb * 512:
                                     jt * D + (nb + 1) * 512],
                                start=(jt == 0), stop=(jt == NJT - 1))
                        PR.add("tensor", t_a, waits=wts,
                               inc=s_t if jt == NJT - 1 else None)
                    a_done_nb[nb] = s_t.v

                if s == 0:
                    xw = [(s_vln, ln_done[NST - 2])]
                elif s == 1:
                    xw = [(s_vln, ln_done[NST - 1])]
                else:
                    xw = [(s_vo, out1_done[s - 2])]
                xl = x_load(NST + s, xw)

                out1_nb = {}
                for nb in range(NBD):
                    def v_o1(eng, s=s, nb=nb):
                        if nb == 0:
                            eng.reciprocal(st_(RL), st_(LSUM0 + s % 2))
                        return eng.scalar_tensor_tensor(
                            out1[s % 2][:, nb * 512:(nb + 1) * 512],
                            fw_ps[nb], st_(RL),
                            xb[s % 2][:, nb * 512:(nb + 1) * 512],
                            ALU.mult, ALU.add)
                    w = [(s_t, a_done_nb[nb])]
                    if nb == 0:
                        w += [xl, (s_sx, exp_done[s])]
                    out1_nb[nb] = PR.add("vector", v_o1, waits=w, inc=s_vo)
                    if s == NST - 1:
                        def v_pp(eng, s=s, nb=nb):
                            return eng.reduce_sum(
                                st_(ABS8 + nb),
                                out1[s % 2][:, nb * 512:(nb + 1) * 512],
                                AXX)
                        PR.add("vector", v_pp, inc=s_vm)

                        def s_pp(eng, s=s, nb=nb):
                            return eng.activation(
                                sqscr[:, 0:512],
                                out1[s % 2][:, nb * 512:(nb + 1) * 512],
                                AF.Square, bias=0.0, scale=1.0,
                                accum_out=st_(ABS8 + 2 + nb))
                        lnp["sq"] = PR.add("scalar", s_pp,
                                           waits=[(s_vo, out1_nb[nb])],
                                           inc=s_ss)
                out1_done[s] = out1_nb[NBD - 1]

            if s >= 1:
                sm1 = s - 1

                wts_nb = {}
                for nb in range(NBD):
                    if live:
                        ow = out1_nb[nb]
                    else:
                        ow = out1_done[NST - 1]
                    wts = [(s_vo, ow)]
                    if nb == 0:
                        wts += [(s_vp, ln2T_evd[sm1]),
                                (s_vq, wprep['quant_last'])]
                    wts_nb[nb] = wts

                ffn_t_nb = {}
                for nb in range(NBD):
                    def t_ffn(eng, sm1=sm1, nb=nb):
                        inst = None
                        for ic in range(NH):
                            inst = eng.matmul(
                                fw_ps[nb],
                                ln2T[:, ic * P:(ic + 1) * P],
                                wqT[:, ic * D + nb * 512:
                                    ic * D + (nb + 1) * 512],
                                start=(ic == 0), stop=(ic == NH - 1))
                        return inst
                    ffn_t_nb[nb] = PR.add("tensor", t_ffn,
                                          waits=wts_nb[nb], inc=s_t)
                ffn_t_done[sm1] = ffn_t_nb[NBD - 1]

                if sm1 == NST - 1:
                    for nb in range(NBD):
                        fwc = [(s_t, ffn_t_nb[nb])]
                        if nb == 0:
                            fwc.append((dsem_o, dsem_o.v))

                        def v_fc(eng, sm1=sm1, nb=nb):
                            return eng.scalar_tensor_tensor(
                                outb[:, nb * 512:(nb + 1) * 512],
                                fw_ps[nb], st_(SCALE),
                                out1[sm1 % 2][:, nb * 512:(nb + 1) * 512],
                                ALU.mult, ALU.add)
                        fd = PR.add("vector", v_fc, waits=fwc, inc=s_vf)

                        def y_c(eng, sm1=sm1, nb=nb):
                            return eng.dma_start(
                                out_d[sm1 * P:(sm1 + 1) * P,
                                      nb * 512:(nb + 1) * 512],
                                outb[:, nb * 512:(nb + 1) * 512])
                        PR.add("sync", y_c, waits=[(s_vf, fd)],
                               inc=dsem_o, inc_amt=16)
                    ffn_done[sm1] = fd
                else:
                    fw = [(s_t, ffn_t_done[sm1]), (dsem_o, dsem_o.v)]

                    def v_final(eng, sm1=sm1):
                        inst = None
                        for nb in range(NBD):
                            inst = eng.scalar_tensor_tensor(
                                outb[:, nb * 512:(nb + 1) * 512],
                                fw_ps[nb], st_(SCALE),
                                out1[sm1 % 2][:, nb * 512:(nb + 1) * 512],
                                ALU.mult, ALU.add)
                        return inst
                    ffn_done[sm1] = PR.add("vector", v_final, waits=fw,
                                           inc=s_vf)

                    def y_out(eng, sm1=sm1):
                        return eng.dma_start(
                            out_d[sm1 * P:(sm1 + 1) * P, :], outb[:, :])
                    PR.add("sync", y_out, waits=[(s_vf, ffn_done[sm1])],
                           inc=dsem_o, inc_amt=16)

        PR.raw_wait("sync", dsem_o, dsem_o.v)

        PR.check()

        with nc.Block() as block:
            @block.tensor
            def _(eng):
                PR.emit(eng, "tensor", gsems.get("tensor"))

            @block.vector
            def _(eng):
                PR.emit(eng, "vector", gsems.get("vector"))

            @block.scalar
            def _(eng):
                PR.emit(eng, "scalar", gsems.get("scalar"))

            @block.sync
            def _(eng):
                PR.emit(eng, "sync", gsems.get("sync"))

            @block.gpsimd
            def _(eng):
                PR.emit(eng, "gpsimd", gsems.get("gpsimd"))

    return nc


_NC_CACHE = {}


def get_nc(S=4096):
    if S not in _NC_CACHE:
        _NC_CACHE[S] = build_nc(S)
    return _NC_CACHE[S]


def kernel(L0_attn_x, L0_attn_W, L0_attn_b, L0_ffn_weight):
    x = np.ascontiguousarray(np.asarray(L0_attn_x))
    W = np.asarray(L0_attn_W)
    b = np.ascontiguousarray(np.asarray(L0_attn_b))
    wf = np.asarray(L0_ffn_weight)
    wt = np.ascontiguousarray(W.T)
    ft = np.ascontiguousarray(wf.T)

    B, S, Dm = x.shape
    assert Dm == D and B == 8
    nc = get_nc(S)
    in_maps = [{"x": x[c], "wt": wt, "b": b, "ft": ft} for c in range(B)]
    res = run_bass_kernel_spmd(nc, in_maps, core_ids=list(range(B)))
    return np.stack([np.asarray(r["out"]) for r in res.results], axis=0)
